# version 01
# baseline (speedup 1.0000x reference)
import contextlib
import numpy as np
import ml_dtypes
import concourse.bass as bass
import concourse.mybir as mybir
from concourse.bass_utils import run_bass_kernel_spmd
from concourse.libnrt import physical_to_logical_core_id

F32 = mybir.dt.float32
BF16 = mybir.dt.bfloat16
AF = mybir.ActivationFunctionType
OP = mybir.AluOpType

NCORE = 8
L = 8192
D = 2048
E = 4096
N = 16
R = 128
I = 8192
LC = L // NCORE
EC = E // NCORE
NET = EC // 128
CH = 1024
NCH = L // CH
EPS = 1e-5

ENGINES = ("sync", "gpsimd", "vector", "scalar", "tensor")


class P:

    def __init__(self, nc, stack, semnames):
        self.nc = nc
        self.prog = {e: [] for e in ENGINES}
        self.sems = {s: stack.enter_context(nc.semaphore(s)) for s in semnames}
        self.c = {s: 0 for s in semnames}
        self._last = {}
        self.nbar = 0

    def wait(self, eng, sem, val):
        if val is None or val <= 0:
            return
        self.prog[eng].append(lambda e, s=sem, v=val: e.wait_ge(self.sems[s], v))

    CHAIN = {"vector": "vv", "scalar": "aa", "gpsimd": "gg"}

    def emit(self, eng, builder, inc=None, inck=1):
        chain_sem = self.CHAIN.get(eng)
        dma_like = inc in ("dgp", "dsy") or (inc or "").startswith("cc")
        if chain_sem is not None and not dma_like:
            self.wait(eng, chain_sem, self._last.get(eng, 0))
            if inc is None:
                inc, inck = chain_sem, 1
            assert inc == chain_sem, (eng, inc)
        if inc is not None:
            self.c[inc] += inck
            v = self.c[inc]

            def fn(e, b=builder, s=inc, k=inck):
                b(e).then_inc(self.sems[s], k)
            self.prog[eng].append(fn)
            if chain_sem is not None and not dma_like:
                self._last[eng] = v
            return v
        self.prog[eng].append(lambda e, b=builder: b(e))
        return None

    def ack(self, eng, qsem, asem):
        self.wait(eng, qsem, self.c[qsem])
        self.c[asem] += 1
        v = self.c[asem]
        self.prog[eng].append(lambda e, s=asem: e.sem_inc(self.sems[s], 1))
        return v

    def barrier(self, dma_flush=()):
        self.nbar += 1
        target = 5 * self.nbar
        flush = {}
        for eng, s in dma_flush:
            flush.setdefault(eng, []).append((s, self.c[s]))
        for eng in ENGINES:
            for s, v in flush.get(eng, []):
                self.wait(eng, s, v)
            self.prog[eng].append(lambda e: e.sem_inc(self.sems["bar"], 1))
            self.wait(eng, "bar", target)


def build_nc(debug=False, preamble="clear_sled"):
    nc = bass.Bass("TRN2", target_bir_lowering=False)

    def din(name, shape, dtype=F32):
        return nc.dram_tensor(name, shape, dtype, kind="ExternalInput").ap()

    resid = din("resid", [LC, D])
    nw_rep = din("nw_rep", [128, D])
    inwT = din("inwT", [D, EC])
    skwT = din("skwT", [D, EC])
    wd1T = din("wd1T", [EC, R])
    wd2T = din("wd2T", [R, EC])
    wbcT = din("wbcT", [EC, 32])
    evec = din("evec", [EC, 24])
    sel_in = din("sel_in", [32, 32 * 128], BF16)
    wguT_sl = din("wguT_sl", [D // NCORE, 2 * I])
    wdnT_sl = din("wdnT_sl", [I // NCORE, D])
    owT_sl = din("owT_sl", [E // NCORE, D])
    out = nc.dram_tensor("out", [LC, D], F32, kind="ExternalOutput").ap()

    ag_rn_in = nc.dram_tensor("ag_rn_in", [LC, D], BF16)
    RN_ALL = nc.dram_tensor("RN_ALL", [L, D], BF16)
    ag_wgu_in = nc.dram_tensor("ag_wgu_in", [D // NCORE, 2 * I], BF16)
    WGUT = nc.dram_tensor("WGUT", [D, 2 * I], BF16)
    ag_wdn_in = nc.dram_tensor("ag_wdn_in", [I // NCORE, D], BF16)
    WDNT = nc.dram_tensor("WDNT", [I, D], BF16)
    ag_ow_in = nc.dram_tensor("ag_ow_in", [E // NCORE, D], BF16)
    OWT = nc.dram_tensor("OWT", [E, D], BF16)
    ar_in = nc.dram_tensor("ar_in", [R + 32, L], F32)
    ar_out = nc.dram_tensor("ar_out", [R + 32, L], F32)
    ag_y_in = nc.dram_tensor("ag_y_in", [EC, L], BF16)
    Y_ALL = nc.dram_tensor("Y_ALL", [E, L], BF16)
    XN_SP = nc.dram_tensor("XN_SP", [LC, D], F32)
    XNBF_SP = nc.dram_tensor("XNBF_SP", [LC, D], BF16)
    SK2_SP = nc.dram_tensor("SK2_SP", [EC, L], F32)
    OUT2_SP = nc.dram_tensor("OUT2_SP", [LC, D], F32)
    X_SP = nc.dram_tensor("X_SP", [EC, L], BF16)

    dbg = {}
    if debug:
        for nm, shape, dt in [
            ("dbg_xn", [LC, D], F32), ("dbg_rn", [LC, D], BF16),
            ("dbg_d1bc", [R + 32, L], F32), ("dbg_x", [EC, L], BF16),
            ("dbg_sk2", [EC, L], F32), ("dbg_delta", [EC, L], F32),
            ("dbg_y", [EC, L], BF16), ("dbg_out2", [LC, D], F32),
            ("dbg_scan", [128, 6 * CH], F32),
        ]:
            dbg[nm] = nc.dram_tensor(nm, shape, dt, kind="ExternalOutput").ap()

    semnames = ["bar", "dsy", "dgp", "asy", "agp", "mm", "vv", "aa", "gg",
                "cc_rn", "cc_wgu", "cc_wdn", "cc_ow", "cc_ar", "cc_y"]

    with contextlib.ExitStack() as g:
        p = P(nc, g, semnames)
        W, EM = p.wait, p.emit

        def gack():
            return p.ack("gpsimd", "dgp", "agp")

        def sack():
            return p.ack("sync", "dsy", "asy")

        nums = sorted(s.num for s in p.sems.values())
        assert nums == list(range(nums[0], nums[0] + len(nums))), nums
        if preamble == "clear_sled":
            for _ in range(40):
                nc.gpsimd.nop(cycle_cnt=2400)
            nc.gpsimd.sem_clear(range(nums[0], nums[-1] + 1))
            for eng in (nc.sync, nc.vector, nc.scalar, nc.tensor):
                for _ in range(120):
                    eng.nop(cycle_cnt=2400)
            nc.all_engine_barrier()

        sbg = lambda name, shape, dtype=F32: g.enter_context(nc.sbuf_tensor(name, shape, dtype))
        evs = sbg("evs", [128, NET * 24])
        asb = sbg("asb", [128, NET * 16])
        carry = sbg("carry", [128, NET * N])

        def ecol(et, c):
            return evs[:, et * 24 + c:et * 24 + c + 1]

        with contextlib.ExitStack() as ph:
            sb = lambda name, shape, dtype=F32: ph.enter_context(nc.sbuf_tensor(name, shape, dtype))
            nwr = sb("nwr", [128, D])
            rsd = [sb(f"rsd{i}", [128, D]) for i in range(2)]
            sq_scr = sb("sq_scr", [128, D])
            xn_t = [sb(f"xn_t{i}", [128, D]) for i in range(2)]
            rn_bf = [sb(f"rn_bf{i}", [128, D], BF16) for i in range(2)]
            xn_bf = [sb(f"xn_bf{i}", [128, D], BF16) for i in range(2)]
            stat = sb("stat", [128, 8])
            wf32 = sb("wf32", [128, 4096])
            wbf = sb("wbf", [128, 4096], BF16)

            EM("gpsimd", lambda e: e.dma_start(out=nwr[:, :], in_=nw_rep), inc="dgp", inck=16)
            EM("gpsimd", lambda e: e.dma_start(
                out=evs[:, :].rearrange("p (t c) -> p t c", t=NET),
                in_=evec.rearrange("(t p) c -> p t c", p=128)), inc="dgp", inck=16)
            a_ev = gack()
            W("scalar", "agp", a_ev)
            EM("scalar", lambda e: e.activation(
                asb[:, :].rearrange("p (t n) -> p t n", t=NET),
                evs[:, :].rearrange("p (t c) -> p t c", t=NET)[:, :, 8:24], AF.Exp), inc="aa")
            W("vector", "aa", p.c["aa"])
            EM("vector", lambda e: e.tensor_scalar_mul(asb[:, :], asb[:, :], -1.0), inc="vv")

            rsd_free = {}
            spill_ack = {}
            for t in range(8):
                b = t % 2
                if t >= 2:
                    W("gpsimd", "vv", rsd_free[t - 2])
                EM("gpsimd", lambda e, b=b, t=t: e.dma_start(
                    out=rsd[b][:, :], in_=resid[t * 128:(t + 1) * 128, :]), inc="dgp", inck=16)
                a_ld = gack()
                W("vector", "agp", a_ld)
                rs1 = EM("vector", lambda e, b=b: e.reduce_sum(
                    stat[:, 0:1], rsd[b][:, :], axis=mybir.AxisListType.X), inc="vv")
                W("vector", "vv", rs1)
                EM("vector", lambda e: e.tensor_scalar_mul(stat[:, 0:1], stat[:, 0:1], 1.0 / D))
                sqv = EM("vector", lambda e, b=b: e.tensor_mul(sq_scr[:, :], rsd[b][:, :], rsd[b][:, :]),
                         inc="vv")
                W("vector", "vv", sqv)
                rs2 = EM("vector", lambda e: e.reduce_sum(
                    stat[:, 1:2], sq_scr[:, :], axis=mybir.AxisListType.X), inc="vv")
                W("vector", "vv", rs2)
                ts2 = EM("vector", lambda e: e.tensor_scalar(
                    stat[:, 2:3], stat[:, 1:2], 1.0 / D, EPS, OP.mult, OP.add), inc="vv")
                EM("vector", lambda e: e.tensor_mul(stat[:, 7:8], stat[:, 0:1], stat[:, 0:1]))
                W("vector", "vv", ts2)
                vv1 = EM("vector", lambda e: e.tensor_sub(stat[:, 2:3], stat[:, 2:3], stat[:, 7:8]), inc="vv")
                W("scalar", "vv", vv1)
                EM("scalar", lambda e: e.activation(stat[:, 3:4], stat[:, 2:3], AF.Ln))
                aa1 = EM("scalar", lambda e: e.activation(stat[:, 3:4], stat[:, 3:4], AF.Exp, scale=-0.5), inc="aa")
                W("vector", "aa", aa1)
                if t >= 2:
                    W("vector", "agp", spill_ack[t - 2])
                xnv = EM("vector", lambda e, b=b: e.tensor_scalar(
                    xn_t[b][:, :], rsd[b][:, :], stat[:, 0:1], stat[:, 3:4], OP.subtract, OP.mult), inc="vv")
                rsd_free[t] = xnv
                W("vector", "vv", xnv)
                sqv2 = EM("vector", lambda e, b=b: e.tensor_mul(sq_scr[:, :], xn_t[b][:, :], xn_t[b][:, :]),
                          inc="vv")
                W("vector", "vv", sqv2)
                rs3 = EM("vector", lambda e: e.reduce_sum(
                    stat[:, 4:5], sq_scr[:, :], axis=mybir.AxisListType.X), inc="vv")
                W("vector", "vv", rs3)
                vv2 = EM("vector", lambda e: e.tensor_scalar(
                    stat[:, 5:6], stat[:, 4:5], 1.0 / D, EPS, OP.mult, OP.add), inc="vv")
                W("scalar", "vv", vv2)
                EM("scalar", lambda e: e.activation(stat[:, 6:7], stat[:, 5:6], AF.Ln))
                aa2 = EM("scalar", lambda e: e.activation(stat[:, 6:7], stat[:, 6:7], AF.Exp, scale=-0.5), inc="aa")
                W("vector", "aa", aa2)
                rnv = EM("vector", lambda e, b=b: e.scalar_tensor_tensor(
                    rn_bf[b][:, :], xn_t[b][:, :], stat[:, 6:7], nwr[:, :], OP.mult, OP.mult), inc="vv")
                W("vector", "vv", rnv)
                vd = EM("vector", lambda e, b=b: e.tensor_copy(xn_bf[b][:, :], xn_t[b][:, :]), inc="vv")
                W("gpsimd", "vv", vd)
                EM("gpsimd", lambda e, b=b, t=t: e.dma_start(
                    out=XN_SP.ap()[t * 128:(t + 1) * 128, :], in_=xn_t[b][:, :]), inc="dgp", inck=16)
                EM("gpsimd", lambda e, b=b, t=t: e.dma_start(
                    out=ag_rn_in.ap()[t * 128:(t + 1) * 128, :], in_=rn_bf[b][:, :]), inc="dgp", inck=16)
                EM("gpsimd", lambda e, b=b, t=t: e.dma_start(
                    out=XNBF_SP.ap()[t * 128:(t + 1) * 128, :], in_=xn_bf[b][:, :]), inc="dgp", inck=16)
                spill_ack[t] = gack()

            def cast_chunks(src_ap, dst_dram, nrows, ncols, colchunk):
                a_store = None
                for rt in range(nrows // 128):
                    for c0 in range(0, ncols, colchunk):
                        EM("gpsimd", lambda e, rt=rt, c0=c0, cc=colchunk: e.dma_start(
                            out=wf32[:, 0:cc], in_=src_ap[rt * 128:(rt + 1) * 128, c0:c0 + cc]),
                            inc="dgp", inck=16)
                        a_ld = gack()
                        W("vector", "agp", a_ld)
                        if a_store is not None:
                            W("vector", "agp", a_store)
                        cv = EM("vector", lambda e, cc=colchunk: e.tensor_copy(wbf[:, 0:cc], wf32[:, 0:cc]),
                                inc="vv")
                        W("gpsimd", "vv", cv)
                        EM("gpsimd", lambda e, rt=rt, c0=c0, cc=colchunk: e.dma_start(
                            out=dst_dram.ap()[rt * 128:(rt + 1) * 128, c0:c0 + cc], in_=wbf[:, 0:cc]),
                            inc="dgp", inck=16)
                        a_store = gack()

            cast_chunks(wguT_sl, ag_wgu_in, D // NCORE, 2 * I, 4096)
            cast_chunks(wdnT_sl, ag_wdn_in, I // NCORE, D, 2048)
            cast_chunks(owT_sl, ag_ow_in, E // NCORE, D, 2048)

            W("gpsimd", "dgp", p.c["dgp"])
            for iap, oap, sem in [
                (ag_rn_in, RN_ALL, "cc_rn"), (ag_wgu_in, WGUT, "cc_wgu"),
                (ag_wdn_in, WDNT, "cc_wdn"), (ag_ow_in, OWT, "cc_ow"),
            ]:
                EM("gpsimd", lambda e, iap=iap, oap=oap: e.collective_compute(
                    "AllGather", OP.bypass, replica_groups=[list(range(NCORE))],
                    ins=[iap.ap().opt()], outs=[oap.ap().opt()]), inc=sem)
            p.barrier(dma_flush=[("gpsimd", "dgp")])

        with contextlib.ExitStack() as ph:
            sb = lambda name, shape, dtype=F32: ph.enter_context(nc.sbuf_tensor(name, shape, dtype))
            psb = lambda name, shape: ph.enter_context(nc.psum_tensor(name, shape, F32))
            wstage = sb("wstage", [128, 16 * 512])
            inw_bf = sb("inw_bf", [128, 16 * 512], BF16)
            skw_bf = sb("skw_bf", [128, 16 * 512], BF16)
            wd1_bf = sb("wd1_bf", [128, NET * 128], BF16)
            wbc_bf = sb("wbc_bf", [128, NET * 32], BF16)
            rhs = [sb(f"rhs{i}", [128, 16 * 512], BF16) for i in range(2)]
            xin_buf = [sb(f"xinb{et}", [128, 515]) for et in range(NET)]
            acc = [sb(f"acc{et}", [128, 512]) for et in range(NET)]
            sk2_t = [[sb(f"sk2t{i}_{et}", [128, 512]) for et in range(NET)] for i in range(2)]
            d1e = [sb(f"d1e{i}", [128, 512]) for i in range(2)]
            bce = [sb(f"bce{i}", [128, 512]) for i in range(2)]
            x_bf = sb("x_bf", [128, NET * L], BF16)
            ps_xin = [psb(f"ps_xin{i}", [128, 512]) for i in range(2)]
            ps_skip = [psb(f"ps_skip{i}", [128, 512]) for i in range(2)]
            ps_d1 = [psb(f"ps_d1{i}", [128, 512]) for i in range(2)]

            def load_cast(src, w, sl, nfree):
                EM("gpsimd", lambda e, src=src, sl=sl, nfree=nfree: e.dma_start(
                    out=wstage[:, 0:nfree].rearrange("p (t c) -> p t c", t=sl),
                    in_=src.rearrange("(t p) c -> p t c", p=128)), inc="dgp", inck=16)
                a_ld = gack()
                W("vector", "agp", a_ld)
                cv = EM("vector", lambda e, w=w, nfree=nfree: e.tensor_copy(w[:, 0:nfree], wstage[:, 0:nfree]),
                        inc="vv")
                W("gpsimd", "vv", cv)

            load_cast(inwT, inw_bf, 16, 16 * 512)
            load_cast(skwT, skw_bf, 16, 16 * 512)
            load_cast(wd1T, wd1_bf, NET, NET * 128)
            load_cast(wbcT, wbc_bf, NET, NET * 32)
            for et in range(NET - 1):
                EM("vector", lambda e, et=et: e.memset(xin_buf[et][:, 0:3], 0.0))
            v_w = EM("vector", lambda e: e.memset(xin_buf[NET - 1][:, 0:3], 0.0), inc="vv")

            W("sync", "cc_rn", 1)
            NLC = 16
            mm_xin, mm_skip, evac_xin, evac_skip = {}, {}, {}, {}
            xsilu, sk2_ack = {}, {}
            for lc in range(NLC):
                b = lc % 2
                if lc >= 2:
                    W("sync", "mm", mm_skip[(lc - 2, NET - 1)])
                for dt in range(16):
                    EM("sync", lambda e, lc=lc, dt=dt, b=b: e.dma_start_transpose(
                        rhs[b][:, dt * 512:(dt + 1) * 512],
                        RN_ALL.ap()[lc * 512:(lc + 1) * 512, dt * 128:(dt + 1) * 128]), inc="dsy", inck=16)
                a_rhs = sack()
                W("tensor", "asy", a_rhs)
                if lc == 0:
                    W("tensor", "vv", v_w)
                for et in range(NET):
                    pb = et % 2
                    prev = (lc, et - 2) if et >= 2 else (lc - 1, et + 2)
                    if prev in evac_xin:
                        W("tensor", "vv", evac_xin[prev])
                    for dt in range(16):
                        mm_xin[(lc, et)] = EM("tensor", lambda e, et=et, dt=dt, b=b, pb=pb: e.matmul(
                            ps_xin[pb][:, :], inw_bf[:, dt * 512 + et * 128:dt * 512 + (et + 1) * 128],
                            rhs[b][:, dt * 512:(dt + 1) * 512], start=(dt == 0), stop=(dt == 15)),
                            inc="mm" if dt == 15 else None)
                    if prev in evac_skip:
                        W("tensor", "aa", evac_skip[prev])
                    for dt in range(16):
                        mm_skip[(lc, et)] = EM("tensor", lambda e, et=et, dt=dt, b=b, pb=pb: e.matmul(
                            ps_skip[pb][:, :], skw_bf[:, dt * 512 + et * 128:dt * 512 + (et + 1) * 128],
                            rhs[b][:, dt * 512:(dt + 1) * 512], start=(dt == 0), stop=(dt == 15)),
                            inc="mm" if dt == 15 else None)

                    W("vector", "mm", mm_xin[(lc, et)])
                    if (lc - 1, et) in xsilu:
                        W("vector", "aa", xsilu[(lc - 1, et)])
                    evv = EM("vector", lambda e, et=et, pb=pb: e.tensor_copy(
                        xin_buf[et][:, 3:515], ps_xin[pb][:, :]), inc="vv")
                    W("vector", "vv", evv)
                    cv0 = EM("vector", lambda e, et=et: e.tensor_scalar(
                        acc[et][:, :], xin_buf[et][:, 0:512], ecol(et, 0), None, OP.mult), inc="vv")
                    for k in range(1, 4):
                        W("vector", "vv", cv0)
                        cv0 = EM("vector", lambda e, et=et, k=k: e.scalar_tensor_tensor(
                            acc[et][:, :], xin_buf[et][:, k:k + 512], ecol(et, k),
                            acc[et][:, :], OP.mult, OP.add), inc="vv")
                    evac_xin[(lc, et)] = cv0
                    EM("vector", lambda e, et=et: e.tensor_copy(
                        xin_buf[et][:, 0:3], xin_buf[et][:, 512:515]), inc="vv")
                    W("scalar", "vv", evac_xin[(lc, et)])
                    xsilu[(lc, et)] = EM("scalar", lambda e, et=et, lc=lc: e.activation(
                        x_bf[:, et * L + lc * 512:et * L + (lc + 1) * 512], acc[et][:, :], AF.Silu,
                        bias=ecol(et, 4)), inc="aa")
                    W("scalar", "mm", mm_skip[(lc, et)])
                    if (lc - 2, et) in sk2_ack:
                        W("scalar", "agp", sk2_ack[(lc - 2, et)])
                    evac_skip[(lc, et)] = EM("scalar", lambda e, et=et, pb=pb, b=b: e.activation(
                        sk2_t[b][et][:, :], ps_skip[pb][:, :], AF.Silu), inc="aa")
                    W("gpsimd", "aa", evac_skip[(lc, et)])
                    EM("gpsimd", lambda e, et=et, lc=lc, b=b: e.dma_start(
                        out=SK2_SP.ap()[et * 128:(et + 1) * 128, lc * 512:(lc + 1) * 512],
                        in_=sk2_t[b][et][:, :]), inc="dgp", inck=16)
                    sk2_ack[(lc, et)] = gack()

            x_ready = p.c["aa"]

            d1_done, d1_ack = {}, {}
            W("tensor", "aa", x_ready)
            for lc in range(NLC):
                b = lc % 2
                if lc >= 2:
                    W("tensor", "vv", d1_done[lc - 2])
                m1 = m2 = None
                for et in range(NET):
                    m1 = EM("tensor", lambda e, et=et, lc=lc, b=b: e.matmul(
                        ps_d1[b][:, :], wd1_bf[:, et * 128:(et + 1) * 128],
                        x_bf[:, et * L + lc * 512:et * L + (lc + 1) * 512],
                        start=(et == 0), stop=(et == NET - 1)),
                        inc="mm" if et == NET - 1 else None)
                for et in range(NET):
                    m2 = EM("tensor", lambda e, et=et, lc=lc, b=b: e.matmul(
                        ps_skip[b][0:32, :], wbc_bf[:, et * 32:(et + 1) * 32],
                        x_bf[:, et * L + lc * 512:et * L + (lc + 1) * 512],
                        start=(et == 0), stop=(et == NET - 1)),
                        inc="mm" if et == NET - 1 else None)
                W("vector", "mm", m1)
                if lc >= 2:
                    W("vector", "agp", d1_ack[lc - 2])
                EM("vector", lambda e, b=b: e.tensor_copy(d1e[b][:, :], ps_d1[b][:, :]))
                W("vector", "mm", m2)
                d1_done[lc] = EM("vector", lambda e, b=b: e.tensor_copy(bce[b][0:32, :], ps_skip[b][0:32, :]),
                                 inc="vv")
                W("gpsimd", "vv", d1_done[lc])
                EM("gpsimd", lambda e, lc=lc, b=b: e.dma_start(
                    out=ar_in.ap()[0:R, lc * 512:(lc + 1) * 512], in_=d1e[b][:, :]), inc="dgp", inck=16)
                EM("gpsimd", lambda e, lc=lc, b=b: e.dma_start(
                    out=ar_in.ap()[R:R + 32, lc * 512:(lc + 1) * 512], in_=bce[b][0:32, :]), inc="dgp", inck=16)
                d1_ack[lc] = gack()
            W("gpsimd", "dgp", p.c["dgp"])
            EM("gpsimd", lambda e: e.collective_compute(
                "AllReduce", OP.add, replica_groups=[list(range(NCORE))],
                ins=[ar_in.ap().opt()], outs=[ar_out.ap().opt()]), inc="cc_ar")
            W("gpsimd", "aa", x_ready)
            for et in range(NET):
                EM("gpsimd", lambda e, et=et: e.dma_start(
                    out=X_SP.ap()[et * 128:(et + 1) * 128, :], in_=x_bf[:, et * L:(et + 1) * L]),
                    inc="dgp", inck=16)
            p.barrier(dma_flush=[("gpsimd", "dgp"), ("sync", "dsy")])

        with contextlib.ExitStack() as ph:
            sb = lambda name, shape, dtype=F32: ph.enter_context(nc.sbuf_tensor(name, shape, dtype))
            psb = lambda name, shape: ph.enter_context(nc.psum_tensor(name, shape, F32))
            d1t_f = sb("d1t_f", [128, CH])
            d1t_bf = sb("d1t_bf", [128, CH], BF16)
            bmc_bf = sb("bmc_bf", [32, L], BF16)
            wd2_bf = sb("wd2_bf", [128, EC], BF16)
            delta = sb("delta", [128, NET * CH])
            xc_bf = sb("xc_bf", [128, NET * CH], BF16)
            u = sb("u", [128, NET * CH])
            y_acc = sb("y_acc", [128, NET * CH])
            sk2c = [sb(f"sk2c{i}", [128, CH]) for i in range(2)]
            p_buf = sb("p_buf", [128, 2 * CH])
            ug_buf = sb("ug_buf", [128, 2 * CH])
            h_buf = sb("h_buf", [128, 2 * CH])
            prod = sb("prod", [128, CH])
            brep = sb("brep", [128, 2 * CH])
            crep = sb("crep", [128, 2 * CH])
            t0 = sb("t0", [128, CH])
            t1 = sb("t1", [128, CH])
            y_bf = [sb(f"y_bf{i}", [128, CH], BF16) for i in range(2)]
            sel_bf = sb("sel_bf", [32, 32 * 128], BF16)
            ps_dp = psb("ps_dp", [128, CH])
            ps_b = psb("ps_b", [128, CH])
            ps_c = psb("ps_c", [128, CH])

            EM("gpsimd", lambda e: e.dma_start(out=sel_bf[:, :], in_=sel_in), inc="dgp", inck=16)
            EM("gpsimd", lambda e: e.dma_start(out=d1t_f[:, 0:EC], in_=wd2T), inc="dgp", inck=16)
            a_w = gack()
            W("vector", "agp", a_w)
            wcv = EM("vector", lambda e: e.tensor_copy(wd2_bf[:, :], d1t_f[:, 0:EC]), inc="vv")
            W("sync", "cc_ar", 1)
            W("sync", "vv", wcv)
            for cb in range(NCH):
                EM("sync", lambda e, cb=cb: e.dma_start(
                    out=t0[0:32, 0:CH], in_=ar_out.ap()[R:R + 32, cb * CH:(cb + 1) * CH]), inc="dsy", inck=16)
                a_b = sack()
                W("vector", "asy", a_b)
                bv = EM("vector", lambda e, cb=cb: e.tensor_copy(
                    bmc_bf[:, cb * CH:(cb + 1) * CH], t0[0:32, 0:CH]), inc="vv")
                W("sync", "vv", bv)
            W("tensor", "agp", a_w)

            scan_done, brep_ok, pg_done = {}, {}, {}
            y_ack, fixv = {}, {}
            sk_ack = {}
            delta_ok = {}
            for cnk in range(NCH):
                c0 = cnk * CH
                if cnk >= 1:
                    W("sync", "vv", delta_ok[(cnk - 1, 0)])
                EM("sync", lambda e, c0=c0: e.dma_start(
                    out=d1t_f[:, :], in_=ar_out.ap()[0:R, c0:c0 + CH]), inc="dsy", inck=16)
                a_d1 = sack()
                W("vector", "asy", a_d1)
                dcv = EM("vector", lambda e: e.tensor_copy(d1t_bf[:, :], d1t_f[:, :]), inc="vv")
                W("tensor", "vv", dcv)
                for et in range(NET):
                    mdp = None
                    for hh in range(2):
                        mdp = EM("tensor", lambda e, et=et, hh=hh: e.matmul(
                            ps_dp[:, hh * 512:(hh + 1) * 512], wd2_bf[:, et * 128:(et + 1) * 128],
                            d1t_bf[:, hh * 512:(hh + 1) * 512], start=True, stop=True),
                            inc="mm" if hh == 1 else None)
                    W("scalar", "mm", mdp)
                    if (cnk, et - 1) in delta_ok:
                        W("scalar", "vv", delta_ok[(cnk, et - 1)])
                    elif cnk >= 1:
                        W("scalar", "vv", fixv[(cnk - 1, NET - 1)])
                    EM("scalar", lambda e, et=et: e.activation(t0[:, :], ps_dp[:, :], AF.Relu, bias=ecol(et, 5)))
                    EM("scalar", lambda e, et=et: e.activation(t1[:, :], ps_dp[:, :], AF.Abs, bias=ecol(et, 5)))
                    EM("scalar", lambda e: e.activation(t1[:, :], t1[:, :], AF.Exp, scale=-1.0))
                    sdn = EM("scalar", lambda e: e.activation(t1[:, :], t1[:, :], AF.Ln, bias=1.0), inc="aa")
                    W("tensor", "aa", sdn)
                    W("vector", "aa", sdn)
                    delta_ok[(cnk, et)] = EM("vector", lambda e, et=et: e.tensor_add(
                        delta[:, et * CH:(et + 1) * CH], t0[:, :], t1[:, :]), inc="vv")
                if cnk >= 1:
                    W("gpsimd", "vv", fixv[(cnk - 1, NET - 1)])
                EM("gpsimd", lambda e, c0=c0: e.dma_start(
                    out=xc_bf[:, :].rearrange("p (et l) -> p et l", et=NET),
                    in_=X_SP.ap()[:, c0:c0 + CH].rearrange("(et p) l -> p et l", p=128)),
                    inc="dgp", inck=16)
                a_x = gack()
                if cnk >= 1:
                    W("vector", "gg", pg_done[(cnk - 1, N - 1, NET - 1)])
                W("vector", "agp", a_x)
                for et in range(NET):
                    EM("vector", lambda e, et=et: e.tensor_copy(t1[:, :], xc_bf[:, et * CH:(et + 1) * CH]))
                    EM("vector", lambda e, et=et: e.tensor_mul(
                        u[:, et * CH:(et + 1) * CH], delta[:, et * CH:(et + 1) * CH], t1[:, :]))
                uv = EM("vector", lambda e: e.memset(y_acc[:, :], 0.0), inc="vv")

                for n in range(N):
                    nb = n % 2
                    if (cnk, n - 1) in brep_ok:
                        W("tensor", "aa", brep_ok[(cnk, n - 1)])
                    elif cnk >= 1 and n == 0:
                        W("tensor", "aa", brep_ok[(cnk - 1, N - 1)])
                    for hh in range(2):
                        EM("tensor", lambda e, n=n, hh=hh, c0=c0: e.matmul(
                            ps_b[:, hh * 512:(hh + 1) * 512], sel_bf[0:32, n * 128:(n + 1) * 128],
                            bmc_bf[0:32, c0 + hh * 512:c0 + (hh + 1) * 512], start=True, stop=True))
                    mbc = None
                    for hh in range(2):
                        mbc = EM("tensor", lambda e, n=n, hh=hh, c0=c0: e.matmul(
                            ps_c[:, hh * 512:(hh + 1) * 512], sel_bf[0:32, (16 + n) * 128:(17 + n) * 128],
                            bmc_bf[0:32, c0 + hh * 512:c0 + (hh + 1) * 512], start=True, stop=True),
                            inc="mm" if hh == 1 else None)
                    W("scalar", "mm", mbc)
                    key = (cnk, n - 2, NET - 1) if n >= 2 else ((cnk - 1, N - 2 + n, NET - 1) if cnk >= 1 else None)
                    if key in pg_done:
                        W("scalar", "gg", pg_done[key])
                    skey = (cnk, (n - 2) * NET + NET - 1) if n >= 2 else (
                        (cnk - 1, (N - 2 + n) * NET + NET - 1) if cnk >= 1 else None)
                    if skey in scan_done:
                        W("scalar", "vv", scan_done[skey])
                    EM("scalar", lambda e, nb=nb: e.activation(brep[:, nb * CH:(nb + 1) * CH], ps_b[:, :],
                                                               AF.Identity))
                    bcv = EM("scalar", lambda e, nb=nb: e.activation(crep[:, nb * CH:(nb + 1) * CH], ps_c[:, :],
                                                                     AF.Identity), inc="aa")
                    brep_ok[(cnk, n)] = bcv

                    for et in range(NET):
                        s = n * NET + et
                        sb_ = s % 2
                        pkey = (cnk, s - 2) if s >= 2 else ((cnk - 1, N * NET - 2 + s) if cnk >= 1 else None)
                        W("scalar", "vv", uv)
                        if pkey in scan_done:
                            W("scalar", "vv", scan_done[pkey])
                        pa = EM("scalar", lambda e, et=et, n=n, sb_=sb_: e.activation(
                            p_buf[:, sb_ * CH:(sb_ + 1) * CH], delta[:, et * CH:(et + 1) * CH], AF.Exp,
                            scale=asb[:, et * 16 + n:et * 16 + n + 1]), inc="aa")
                        W("gpsimd", "aa", bcv)
                        W("gpsimd", "vv", uv)
                        if pkey in scan_done:
                            W("gpsimd", "vv", scan_done[pkey])
                        pg = EM("gpsimd", lambda e, et=et, nb=nb, sb_=sb_: e.tensor_mul(
                            ug_buf[:, sb_ * CH:(sb_ + 1) * CH], u[:, et * CH:(et + 1) * CH],
                            brep[:, nb * CH:(nb + 1) * CH]), inc="gg")
                        pg_done[(cnk, n, et)] = pg
                        W("vector", "aa", pa)
                        W("vector", "gg", pg)
                        cidx = et * N + n
                        if cnk == 0:
                            sv0 = EM("vector", lambda e, sb_=sb_: e.tensor_tensor_scan(
                                h_buf[:, sb_ * CH:(sb_ + 1) * CH], p_buf[:, sb_ * CH:(sb_ + 1) * CH],
                                ug_buf[:, sb_ * CH:(sb_ + 1) * CH], 0.0, OP.mult, OP.add), inc="vv")
                        else:
                            sv0 = EM("vector", lambda e, sb_=sb_, cidx=cidx: e.tensor_tensor_scan(
                                h_buf[:, sb_ * CH:(sb_ + 1) * CH], p_buf[:, sb_ * CH:(sb_ + 1) * CH],
                                ug_buf[:, sb_ * CH:(sb_ + 1) * CH], carry[:, cidx:cidx + 1], OP.mult, OP.add),
                                inc="vv")
                        W("vector", "vv", sv0)
                        if debug and cnk == 0 and n == 0 and et == 0:
                            W("gpsimd", "vv", sv0)
                            for j, src_ap in enumerate([
                                    lambda: p_buf[:, 0:CH], lambda: ug_buf[:, 0:CH],
                                    lambda: h_buf[:, 0:CH], lambda: u[:, 0:CH],
                                    lambda: brep[:, 0:CH]]):
                                EM("gpsimd", lambda e, j=j, src_ap=src_ap: e.dma_start(
                                    out=dbg["dbg_scan"][:, j * CH:(j + 1) * CH], in_=src_ap()),
                                    inc="dgp", inck=16)
                            EM("gpsimd", lambda e: e.dma_start(
                                out=dbg["dbg_scan"][:, 5 * CH:6 * CH], in_=crep[:, 0:CH]),
                                inc="dgp", inck=16)
                            scn_a = gack()
                            W("scalar", "agp", scn_a)
                            W("vector", "agp", scn_a)
                            W("gpsimd", "agp", scn_a)
                        EM("vector", lambda e, sb_=sb_, cidx=cidx: e.tensor_copy(
                            carry[:, cidx:cidx + 1], h_buf[:, sb_ * CH + CH - 1:sb_ * CH + CH]))
                        EM("vector", lambda e, sb_=sb_, nb=nb: e.tensor_mul(
                            prod[:, :], h_buf[:, sb_ * CH:(sb_ + 1) * CH], crep[:, nb * CH:(nb + 1) * CH]))
                        scan_done[(cnk, s)] = EM("vector", lambda e, et=et: e.tensor_add(
                            y_acc[:, et * CH:(et + 1) * CH], y_acc[:, et * CH:(et + 1) * CH], prod[:, :]),
                            inc="vv")

                for et in range(NET):
                    yb = et % 2
                    skkey = (cnk, et - 2) if et >= 2 else ((cnk - 1, et + 2) if cnk >= 1 else None)
                    if skkey in fixv:
                        W("gpsimd", "vv", fixv[skkey])
                    EM("gpsimd", lambda e, et=et, c0=c0, yb=yb: e.dma_start(
                        out=sk2c[yb][:, :], in_=SK2_SP.ap()[et * 128:(et + 1) * 128, c0:c0 + CH]),
                        inc="dgp", inck=16)
                    sk_ack[(cnk, et)] = gack()
                    t1v = EM("vector", lambda e, et=et: e.tensor_copy(
                        t1[:, :], xc_bf[:, et * CH:(et + 1) * CH]), inc="vv")
                    W("vector", "vv", t1v)
                    fx0 = EM("vector", lambda e, et=et: e.scalar_tensor_tensor(
                        y_acc[:, et * CH:(et + 1) * CH], t1[:, :], ecol(et, 6),
                        y_acc[:, et * CH:(et + 1) * CH], OP.mult, OP.add), inc="vv")
                    W("vector", "vv", fx0)
                    W("vector", "agp", sk_ack[(cnk, et)])
                    ykey = (cnk, et - 2) if et >= 2 else ((cnk - 1, et + 2) if cnk >= 1 else None)
                    if ykey in y_ack:
                        W("vector", "agp", y_ack[ykey])
                    fixv[(cnk, et)] = EM("vector", lambda e, et=et, yb=yb: e.tensor_mul(
                        y_bf[yb][:, :], y_acc[:, et * CH:(et + 1) * CH], sk2c[yb][:, :]), inc="vv")
                    W("gpsimd", "vv", fixv[(cnk, et)])
                    EM("gpsimd", lambda e, et=et, c0=c0, yb=yb: e.dma_start(
                        out=ag_y_in.ap()[et * 128:(et + 1) * 128, c0:c0 + CH], in_=y_bf[yb][:, :]),
                        inc="dgp", inck=16)
                    y_ack[(cnk, et)] = gack()
                if debug:
                    W("gpsimd", "vv", p.c["vv"])
                    for et in range(NET):
                        EM("gpsimd", lambda e, et=et, c0=c0: e.dma_start(
                            out=dbg["dbg_delta"][et * 128:(et + 1) * 128, c0:c0 + CH],
                            in_=delta[:, et * CH:(et + 1) * CH]), inc="dgp", inck=16)
                    dbg_a = gack()
                    W("vector", "agp", dbg_a)

            W("gpsimd", "dgp", p.c["dgp"])
            EM("gpsimd", lambda e: e.collective_compute(
                "AllGather", OP.bypass, replica_groups=[list(range(NCORE))],
                ins=[ag_y_in.ap().opt()], outs=[Y_ALL.ap().opt()]), inc="cc_y")
            p.barrier(dma_flush=[("gpsimd", "dgp"), ("sync", "dsy")])

        with contextlib.ExitStack() as ph:
            sb = lambda name, shape, dtype=F32: ph.enter_context(nc.sbuf_tensor(name, shape, dtype))
            psb = lambda name, shape: ph.enter_context(nc.psum_tensor(name, shape, F32))
            xnt = sb("xnt", [128, 16 * 512], BF16)
            wg_g = [sb(f"wg_g{i}", [128, 16 * 128], BF16) for i in range(2)]
            wg_u = [sb(f"wg_u{i}", [128, 16 * 128], BF16) for i in range(2)]
            sg = [sb(f"sg{i}", [128, 512]) for i in range(2)]
            h_bf = sb("h_bf", [128, 64 * 512], BF16)
            wdn_t = [sb(f"wdn_t{i}", [128, 512], BF16) for i in range(2)]
            o2_t = sb("o2_t", [128, 512])
            ps_g = [psb(f"ps_g{i}", [128, 512]) for i in range(2)]
            ps_u = [psb(f"ps_u{i}", [128, 512]) for i in range(2)]
            ps_dn = [psb(f"ps_dn{i}", [128, 512]) for i in range(4)]

            W("sync", "cc_wgu", 1)
            W("sync", "cc_wdn", 1)
            gu_mm, h_done, dn_mm, dn_evac = {}, {}, {}, {}
            o2_ack = {}
            for lh in range(2):
                l0 = lh * 512
                if lh == 1:
                    W("sync", "mm", gu_mm[(0, 63)])
                for dt in range(16):
                    EM("sync", lambda e, dt=dt, l0=l0: e.dma_start_transpose(
                        xnt[:, dt * 512:(dt + 1) * 512],
                        XNBF_SP.ap()[l0:l0 + 512, dt * 128:(dt + 1) * 128]), inc="dsy", inck=16)
                a_xnt = sack()
                for it in range(64):
                    b = it % 2
                    if it >= 2:
                        W("sync", "mm", gu_mm[(lh, it - 2)])
                    EM("sync", lambda e, it=it, b=b: e.dma_start(
                        out=wg_g[b][:, :].rearrange("p (dt i) -> p dt i", dt=16),
                        in_=WGUT.ap()[:, it * 128:(it + 1) * 128].rearrange("(dt p) i -> p dt i", p=128)),
                        inc="dsy", inck=16)
                    EM("sync", lambda e, it=it, b=b: e.dma_start(
                        out=wg_u[b][:, :].rearrange("p (dt i) -> p dt i", dt=16),
                        in_=WGUT.ap()[:, I + it * 128:I + (it + 1) * 128].rearrange("(dt p) i -> p dt i", p=128)),
                        inc="dsy", inck=16)
                    a_wg = sack()
                    W("tensor", "asy", max(a_wg, a_xnt))
                    if (lh, it - 2) in h_done:
                        W("tensor", "vv", h_done[(lh, it - 2)])
                    elif lh == 1 and it < 2:
                        W("tensor", "vv", h_done[(0, 62 + it)])
                    for dt in range(16):
                        EM("tensor", lambda e, dt=dt, b=b: e.matmul(
                            ps_g[b][:, :], wg_g[b][:, dt * 128:(dt + 1) * 128], xnt[:, dt * 512:(dt + 1) * 512],
                            start=(dt == 0), stop=(dt == 15)))
                    for dt in range(16):
                        gu_mm[(lh, it)] = EM("tensor", lambda e, dt=dt, b=b: e.matmul(
                            ps_u[b][:, :], wg_u[b][:, dt * 128:(dt + 1) * 128], xnt[:, dt * 512:(dt + 1) * 512],
                            start=(dt == 0), stop=(dt == 15)), inc="mm" if dt == 15 else None)
                    W("scalar", "mm", gu_mm[(lh, it)])
                    if (lh, it - 2) in h_done:
                        W("scalar", "vv", h_done[(lh, it - 2)])
                    elif lh == 1 and it < 2:
                        W("scalar", "vv", h_done[(0, 62 + it)])
                    sa = EM("scalar", lambda e, b=b: e.activation(sg[b][:, :], ps_g[b][:, :], AF.Silu), inc="aa")
                    W("vector", "aa", sa)
                    if lh == 1 and it == 0:
                        W("vector", "mm", dn_mm[(0, 3, 63)])
                    h_done[(lh, it)] = EM("vector", lambda e, it=it, b=b: e.tensor_mul(
                        h_bf[:, it * 512:(it + 1) * 512], sg[b][:, :], ps_u[b][:, :]), inc="vv")
                W("tensor", "vv", h_done[(lh, 63)])
                for dc in range(4):
                    if (lh, dc - 1) in dn_evac:
                        W("tensor", "vv", dn_evac[(lh, dc - 1)])
                    elif lh == 1 and dc == 0:
                        W("tensor", "vv", dn_evac[(0, 3)])
                    for it in range(64):
                        b = it % 2
                        if it >= 2:
                            W("sync", "mm", dn_mm[(lh, dc, it - 2)])
                        elif dc > 0:
                            W("sync", "mm", dn_mm[(lh, dc - 1, 62 + it)])
                        EM("sync", lambda e, it=it, dc=dc, b=b: e.dma_start(
                            out=wdn_t[b][:, :], in_=WDNT.ap()[it * 128:(it + 1) * 128, dc * 512:(dc + 1) * 512]),
                            inc="dsy", inck=16)
                        a_wdn = sack()
                        W("tensor", "asy", a_wdn)
                        for lt2 in range(4):
                            dn_mm[(lh, dc, it)] = EM("tensor", lambda e, it=it, b=b, lt2=lt2: e.matmul(
                                ps_dn[lt2][:, :], h_bf[:, it * 512 + lt2 * 128:it * 512 + (lt2 + 1) * 128],
                                wdn_t[b][:, :], start=(it == 0), stop=(it == 63)),
                                inc="mm" if lt2 == 3 else None)
                    W("vector", "mm", dn_mm[(lh, dc, 63)])
                    ev = None
                    for lt2 in range(4):
                        prev = o2_ack.get((lh, dc, lt2 - 1)) or o2_ack.get((lh, dc - 1, 3)) or \
                            o2_ack.get((lh - 1, 3, 3))
                        if prev:
                            W("vector", "agp", prev)
                        ev = EM("vector", lambda e, lt2=lt2: e.tensor_copy(o2_t[:, :], ps_dn[lt2][:, :]),
                                inc="vv")
                        W("gpsimd", "vv", ev)
                        EM("gpsimd", lambda e, lh=lh, lt2=lt2, dc=dc: e.dma_start(
                            out=OUT2_SP.ap()[lh * 512 + lt2 * 128:lh * 512 + (lt2 + 1) * 128,
                                             dc * 512:(dc + 1) * 512], in_=o2_t[:, :]), inc="dgp", inck=16)
                        o2_ack[(lh, dc, lt2)] = gack()
                    dn_evac[(lh, dc)] = ev
            p.barrier(dma_flush=[("gpsimd", "dgp"), ("sync", "dsy")])

        with contextlib.ExitStack() as ph:
            sb = lambda name, shape, dtype=F32: ph.enter_context(nc.sbuf_tensor(name, shape, dtype))
            psb = lambda name, shape: ph.enter_context(nc.psum_tensor(name, shape, F32))
            y_sb = sb("y_sb", [128, 32 * LC], BF16)
            owt_t = [sb(f"owt_t{i}", [128, 512], BF16) for i in range(2)]
            xn_t4 = sb("xn_t4", [128, 512])
            o2_t4 = sb("o2_t4", [128, 512])
            g_t4 = sb("g_t4", [128, 512])
            rs_t4 = sb("rs_t4", [128, 512])
            s_t4 = sb("s_t4", [128, 512])
            out_t4 = sb("out_t4", [128, 512])
            ps_o = [psb(f"ps_o{i}", [128, 512]) for i in range(8)]

            W("sync", "cc_y", 1)
            W("sync", "cc_ow", 1)

            def y_load(e):
                for arm in e.switch_core_id(n=128):
                    lcore = physical_to_logical_core_id(int(arm) % 128, 0, container_num_cores=NCORE) % NCORE
                    c0 = lcore * LC
                    e.dma_start(
                        out=y_sb[:, :].rearrange("p (et l) -> p et l", et=32),
                        in_=Y_ALL.ap()[:, c0:c0 + LC].rearrange("(et p) l -> p et l", p=128),
                    ).then_inc(p.sems["dsy"], 16)
            p.c["dsy"] += 16
            p.prog["sync"].append(y_load)
            yl_ack = sack()

            ow_mm, ph3_evac, od_ack = {}, {}, {}
            for dc in range(4):
                for et8 in range(32):
                    b = et8 % 2
                    if (dc, et8 - 2) in ow_mm:
                        W("sync", "mm", ow_mm[(dc, et8 - 2)])
                    elif dc > 0 and et8 < 2:
                        W("sync", "mm", ow_mm[(dc - 1, 30 + et8)])
                    EM("sync", lambda e, et8=et8, dc=dc, b=b: e.dma_start(
                        out=owt_t[b][:, :], in_=OWT.ap()[et8 * 128:(et8 + 1) * 128, dc * 512:(dc + 1) * 512]),
                        inc="dsy", inck=16)
                    a_ow = sack()
                    W("tensor", "asy", max(a_ow, yl_ack))
                    if dc > 0 and et8 == 0:
                        W("tensor", "vv", ph3_evac[dc - 1])
                    for lt in range(8):
                        ow_mm[(dc, et8)] = EM("tensor", lambda e, et8=et8, b=b, lt=lt: e.matmul(
                            ps_o[lt][:, :], y_sb[:, et8 * LC + lt * 128:et8 * LC + (lt + 1) * 128],
                            owt_t[b][:, :], start=(et8 == 0), stop=(et8 == 31)),
                            inc="mm" if lt == 7 else None)
                last_ov = None
                for lt in range(8):
                    prev_ov = last_ov if lt > 0 else (ph3_evac.get(dc - 1) if dc > 0 else None)
                    if prev_ov:
                        W("gpsimd", "vv", prev_ov)
                    W("gpsimd", "mm", ow_mm[(dc, 31)])
                    EM("gpsimd", lambda e, lt=lt, dc=dc: e.dma_start(
                        out=xn_t4[:, :], in_=XN_SP.ap()[lt * 128:(lt + 1) * 128, dc * 512:(dc + 1) * 512]),
                        inc="dgp", inck=16)
                    EM("gpsimd", lambda e, lt=lt, dc=dc: e.dma_start(
                        out=o2_t4[:, :], in_=OUT2_SP.ap()[lt * 128:(lt + 1) * 128, dc * 512:(dc + 1) * 512]),
                        inc="dgp", inck=16)
                    EM("gpsimd", lambda e, lt=lt, dc=dc: e.dma_start(
                        out=rs_t4[:, :], in_=resid[lt * 128:(lt + 1) * 128, dc * 512:(dc + 1) * 512]),
                        inc="dgp", inck=16)
                    a_ld3 = gack()
                    W("scalar", "agp", a_ld3)
                    sg4 = EM("scalar", lambda e: e.activation(g_t4[:, :], o2_t4[:, :], AF.Silu), inc="aa")
                    W("vector", "agp", a_ld3)
                    W("vector", "aa", sg4)
                    W("vector", "mm", ow_mm[(dc, 31)])
                    prev_od = od_ack.get((dc, lt - 1)) or (od_ack.get((dc - 1, 7)) if dc > 0 else None)
                    if prev_od:
                        W("vector", "agp", prev_od)
                    EM("vector", lambda e, lt=lt: e.tensor_add(s_t4[:, :], xn_t4[:, :], ps_o[lt][:, :]))
                    EM("vector", lambda e: e.tensor_mul(s_t4[:, :], s_t4[:, :], g_t4[:, :]))
                    last_ov = EM("vector", lambda e: e.tensor_add(out_t4[:, :], rs_t4[:, :], s_t4[:, :]), inc="vv")
                    W("gpsimd", "vv", last_ov)
                    EM("gpsimd", lambda e, lt=lt, dc=dc: e.dma_start(
                        out=out[lt * 128:(lt + 1) * 128, dc * 512:(dc + 1) * 512], in_=out_t4[:, :]),
                        inc="dgp", inck=16)
                    od_ack[(dc, lt)] = gack()
                ph3_evac[dc] = last_ov

            if debug:
                for src, dst in [(XN_SP, dbg["dbg_xn"]), (ag_rn_in, dbg["dbg_rn"]),
                                 (ar_out, dbg["dbg_d1bc"]), (SK2_SP, dbg["dbg_sk2"]),
                                 (ag_y_in, dbg["dbg_y"]), (OUT2_SP, dbg["dbg_out2"]),
                                 (X_SP, dbg["dbg_x"])]:
                    EM("gpsimd", lambda e, src=src, dst=dst: e.dma_start(out=dst, in_=src.ap()),
                       inc="dgp", inck=16)
            W("gpsimd", "dgp", p.c["dgp"])
            p.barrier(dma_flush=[("gpsimd", "dgp"), ("sync", "dsy")])

        return _finish(nc, p)


def _finish(nc, p):
    with nc.Block() as block:
        @block.sync
        def _(e):
            for fn in p.prog["sync"]:
                fn(e)

        @block.gpsimd
        def _(e):
            for fn in p.prog["gpsimd"]:
                fn(e)

        @block.vector
        def _(e):
            for fn in p.prog["vector"]:
                fn(e)

        @block.scalar
        def _(e):
            for fn in p.prog["scalar"]:
                fn(e)

        @block.tensor
        def _(e):
            for fn in p.prog["tensor"]:
                fn(e)
    return nc


def make_in_maps(inputs):
    x0 = np.asarray(inputs["L0_attn_resid"]).reshape(L, D)
    norm_w = np.asarray(inputs["L0_attn_norm_w"])
    skip_w = np.asarray(inputs["L0_attn_skip_w"])
    in_w = np.asarray(inputs["L0_attn_in_w"])
    conv_w = np.asarray(inputs["L0_attn_conv_w"])
    conv_b = np.asarray(inputs["L0_attn_conv_b"])
    wd1 = np.asarray(inputs["L0_attn_wd1"])
    wd2 = np.asarray(inputs["L0_attn_wd2"])
    wd2_b = np.asarray(inputs["L0_attn_wd2_b"])
    wB = np.asarray(inputs["L0_attn_wB"])
    wC = np.asarray(inputs["L0_attn_wC"])
    A_log = np.asarray(inputs["L0_attn_A_log"])
    W_D = np.asarray(inputs["L0_attn_W_D"])
    out_w = np.asarray(inputs["L0_attn_out_w"])
    wgu = np.asarray(inputs["L0_ffn_w_gate_up"])
    wdn = np.asarray(inputs["L0_ffn_w_down"])

    nw_rep = np.ascontiguousarray(np.broadcast_to(norm_w[None, :], (128, D)).astype(np.float32))
    wguT = np.ascontiguousarray(wgu.T)
    wdnT = np.ascontiguousarray(wdn.T)
    owT = np.ascontiguousarray(out_w.T)
    sel = np.zeros((32, 32 * 128), np.float32)
    for k in range(32):
        sel[k, k * 128:(k + 1) * 128] = 1.0
    sel_bf = sel.astype(ml_dtypes.bfloat16)

    in_maps = []
    for i in range(NCORE):
        ls = slice(i * LC, (i + 1) * LC)
        es = slice(i * EC, (i + 1) * EC)
        ev = np.zeros((EC, 24), np.float32)
        ev[:, 0:4] = conv_w[es]
        ev[:, 4] = conv_b[es]
        ev[:, 5] = wd2_b[es]
        ev[:, 6] = W_D[es]
        ev[:, 8:24] = A_log[es]
        in_maps.append({
            "resid": np.ascontiguousarray(x0[ls]),
            "nw_rep": nw_rep,
            "inwT": np.ascontiguousarray(in_w[es].T),
            "skwT": np.ascontiguousarray(skip_w[es].T),
            "wd1T": np.ascontiguousarray(wd1[:, es].T),
            "wd2T": np.ascontiguousarray(wd2[es].T),
            "wbcT": np.ascontiguousarray(np.concatenate([wB[:, es].T, wC[:, es].T], axis=1)),
            "evec": ev,
            "sel_in": sel_bf,
            "wguT_sl": np.ascontiguousarray(wguT[i * (D // NCORE):(i + 1) * (D // NCORE)]),
            "wdnT_sl": np.ascontiguousarray(wdnT[i * (I // NCORE):(i + 1) * (I // NCORE)]),
            "owT_sl": np.ascontiguousarray(owT[i * (E // NCORE):(i + 1) * (E // NCORE)]),
        })
    return in_maps


def kernel(**inputs):
    in_maps = make_in_maps(inputs)
    nc = build_nc()
    res = run_bass_kernel_spmd(nc, in_maps, core_ids=list(range(NCORE)))
    outp = np.concatenate([res.results[i]["out"] for i in range(NCORE)], axis=0)
    return outp.reshape(1, L, D).astype(np.float32)


# version 02
# speedup vs baseline: 1.0342x; 1.0342x over previous
import contextlib
import numpy as np
import ml_dtypes
import concourse.bass as bass
import concourse.mybir as mybir
from concourse.bass_utils import run_bass_kernel_spmd
from concourse.libnrt import physical_to_logical_core_id

F32 = mybir.dt.float32
BF16 = mybir.dt.bfloat16
AF = mybir.ActivationFunctionType
OP = mybir.AluOpType

NCORE = 8
L = 8192
D = 2048
E = 4096
N = 16
R = 128
I = 8192
LC = L // NCORE
EC = E // NCORE
NET = EC // 128
CH = 1024
NCH = L // CH
EPS = 1e-5

ENGINES = ("sync", "gpsimd", "vector", "scalar", "tensor")


class P:

    def __init__(self, nc, stack, semnames):
        self.nc = nc
        self.prog = {e: [] for e in ENGINES}
        self.sems = {s: stack.enter_context(nc.semaphore(s)) for s in semnames}
        self.c = {s: 0 for s in semnames}
        self._last = {}
        self.nbar = 0

    def wait(self, eng, sem, val):
        if val is None or val <= 0:
            return
        self.prog[eng].append(lambda e, s=sem, v=val: e.wait_ge(self.sems[s], v))

    CHAIN = {"vector": "vv", "scalar": "aa", "gpsimd": "gg"}

    def emit(self, eng, builder, inc=None, inck=1):
        chain_sem = self.CHAIN.get(eng)
        dma_like = inc in ("dgp", "dsy") or (inc or "").startswith("cc")
        if chain_sem is not None and not dma_like:
            self.wait(eng, chain_sem, self._last.get(eng, 0))
            if inc is None:
                inc, inck = chain_sem, 1
            assert inc == chain_sem, (eng, inc)
        if inc is not None:
            self.c[inc] += inck
            v = self.c[inc]

            def fn(e, b=builder, s=inc, k=inck):
                b(e).then_inc(self.sems[s], k)
            self.prog[eng].append(fn)
            if chain_sem is not None and not dma_like:
                self._last[eng] = v
            return v
        self.prog[eng].append(lambda e, b=builder: b(e))
        return None

    def ack(self, eng, qsem, asem):
        self.wait(eng, qsem, self.c[qsem])
        self.c[asem] += 1
        v = self.c[asem]
        self.prog[eng].append(lambda e, s=asem: e.sem_inc(self.sems[s], 1))
        return v

    def barrier(self, dma_flush=()):
        self.nbar += 1
        target = 5 * self.nbar
        flush = {}
        for eng, s in dma_flush:
            flush.setdefault(eng, []).append((s, self.c[s]))
        for eng in ENGINES:
            for s, v in flush.get(eng, []):
                self.wait(eng, s, v)
            self.prog[eng].append(lambda e: e.sem_inc(self.sems["bar"], 1))
            self.wait(eng, "bar", target)


def build_nc(debug=False, preamble="clear_sled"):
    nc = bass.Bass("TRN2", target_bir_lowering=False)

    def din(name, shape, dtype=F32):
        return nc.dram_tensor(name, shape, dtype, kind="ExternalInput").ap()

    resid = din("resid", [LC, D])
    nw_rep = din("nw_rep", [128, D])
    inwT = din("inwT", [D, EC])
    skwT = din("skwT", [D, EC])
    wd1T = din("wd1T", [EC, R])
    wd2T = din("wd2T", [R, EC])
    wbcT = din("wbcT", [EC, 32])
    evec = din("evec", [EC, 24])
    sel_in = din("sel_in", [32, 32 * 128], BF16)
    wguT_sl = din("wguT_sl", [D // NCORE, 2 * I])
    wdnT_sl = din("wdnT_sl", [I // NCORE, D])
    owT_sl = din("owT_sl", [E // NCORE, D])
    out = nc.dram_tensor("out", [LC, D], F32, kind="ExternalOutput").ap()

    ag_rn_in = nc.dram_tensor("ag_rn_in", [LC, D], BF16)
    RN_ALL = nc.dram_tensor("RN_ALL", [L, D], BF16, addr_space="Shared")
    ag_wgu_in = nc.dram_tensor("ag_wgu_in", [D // NCORE, 2 * I], BF16)
    WGUT = nc.dram_tensor("WGUT", [D, 2 * I], BF16, addr_space="Shared")
    ag_wdn_in = nc.dram_tensor("ag_wdn_in", [I // NCORE, D], BF16)
    WDNT = nc.dram_tensor("WDNT", [I, D], BF16, addr_space="Shared")
    ag_ow_in = nc.dram_tensor("ag_ow_in", [E // NCORE, D], BF16)
    OWT = nc.dram_tensor("OWT", [E, D], BF16, addr_space="Shared")
    ar_in = nc.dram_tensor("ar_in", [R + 32, L], F32)
    ar_out = nc.dram_tensor("ar_out", [R + 32, L], F32, addr_space="Shared")
    ag_y_in = nc.dram_tensor("ag_y_in", [EC, L], BF16)
    Y_ALL = nc.dram_tensor("Y_ALL", [E, L], BF16, addr_space="Shared")
    XN_SP = nc.dram_tensor("XN_SP", [LC, D], F32)
    XNBF_SP = nc.dram_tensor("XNBF_SP", [LC, D], BF16)
    SK2_SP = nc.dram_tensor("SK2_SP", [EC, L], F32)
    OUT2_SP = nc.dram_tensor("OUT2_SP", [LC, D], F32)
    X_SP = nc.dram_tensor("X_SP", [EC, L], BF16)

    dbg = {}
    if debug:
        for nm, shape, dt in [
            ("dbg_xn", [LC, D], F32), ("dbg_rn", [LC, D], BF16),
            ("dbg_d1bc", [R + 32, L], F32), ("dbg_x", [EC, L], BF16),
            ("dbg_sk2", [EC, L], F32), ("dbg_delta", [EC, L], F32),
            ("dbg_y", [EC, L], BF16), ("dbg_out2", [LC, D], F32),
            ("dbg_scan", [128, 6 * CH], F32),
        ]:
            dbg[nm] = nc.dram_tensor(nm, shape, dt, kind="ExternalOutput").ap()

    semnames = ["bar", "dsy", "dgp", "asy", "agp", "mm", "vv", "aa", "gg",
                "cc_rn", "cc_wgu", "cc_wdn", "cc_ow", "cc_ar", "cc_y"]

    with contextlib.ExitStack() as g:
        p = P(nc, g, semnames)
        W, EM = p.wait, p.emit

        def gack():
            return p.ack("gpsimd", "dgp", "agp")

        def sack():
            return p.ack("sync", "dsy", "asy")

        nums = sorted(s.num for s in p.sems.values())
        assert nums == list(range(nums[0], nums[0] + len(nums))), nums
        if preamble == "clear_sled":
            for _ in range(40):
                nc.gpsimd.nop(cycle_cnt=2400)
            nc.gpsimd.sem_clear(range(nums[0], nums[-1] + 1))
            for eng in (nc.sync, nc.vector, nc.scalar, nc.tensor):
                for _ in range(120):
                    eng.nop(cycle_cnt=2400)
            nc.all_engine_barrier()

        sbg = lambda name, shape, dtype=F32: g.enter_context(nc.sbuf_tensor(name, shape, dtype))
        evs = sbg("evs", [128, NET * 24])
        asb = sbg("asb", [128, NET * 16])
        carry = sbg("carry", [128, NET * N])

        def ecol(et, c):
            return evs[:, et * 24 + c:et * 24 + c + 1]

        with contextlib.ExitStack() as ph:
            sb = lambda name, shape, dtype=F32: ph.enter_context(nc.sbuf_tensor(name, shape, dtype))
            nwr = sb("nwr", [128, D])
            rsd = [sb(f"rsd{i}", [128, D]) for i in range(2)]
            sq_scr = sb("sq_scr", [128, D])
            xn_t = [sb(f"xn_t{i}", [128, D]) for i in range(2)]
            rn_bf = [sb(f"rn_bf{i}", [128, D], BF16) for i in range(2)]
            xn_bf = [sb(f"xn_bf{i}", [128, D], BF16) for i in range(2)]
            stat = sb("stat", [128, 8])
            wf32 = sb("wf32", [128, 4096])
            wbf = sb("wbf", [128, 4096], BF16)

            EM("gpsimd", lambda e: e.dma_start(out=nwr[:, :], in_=nw_rep), inc="dgp", inck=16)
            EM("gpsimd", lambda e: e.dma_start(
                out=evs[:, :].rearrange("p (t c) -> p t c", t=NET),
                in_=evec.rearrange("(t p) c -> p t c", p=128)), inc="dgp", inck=16)
            a_ev = gack()
            W("scalar", "agp", a_ev)
            EM("scalar", lambda e: e.activation(
                asb[:, :].rearrange("p (t n) -> p t n", t=NET),
                evs[:, :].rearrange("p (t c) -> p t c", t=NET)[:, :, 8:24], AF.Exp), inc="aa")
            W("vector", "aa", p.c["aa"])
            EM("vector", lambda e: e.tensor_scalar_mul(asb[:, :], asb[:, :], -1.0), inc="vv")

            rsd_free = {}
            spill_ack = {}
            for t in range(8):
                b = t % 2
                if t >= 2:
                    W("gpsimd", "vv", rsd_free[t - 2])
                EM("gpsimd", lambda e, b=b, t=t: e.dma_start(
                    out=rsd[b][:, :], in_=resid[t * 128:(t + 1) * 128, :]), inc="dgp", inck=16)
                a_ld = gack()
                W("vector", "agp", a_ld)
                rs1 = EM("vector", lambda e, b=b: e.reduce_sum(
                    stat[:, 0:1], rsd[b][:, :], axis=mybir.AxisListType.X), inc="vv")
                W("vector", "vv", rs1)
                EM("vector", lambda e: e.tensor_scalar_mul(stat[:, 0:1], stat[:, 0:1], 1.0 / D))
                sqv = EM("vector", lambda e, b=b: e.tensor_mul(sq_scr[:, :], rsd[b][:, :], rsd[b][:, :]),
                         inc="vv")
                W("vector", "vv", sqv)
                rs2 = EM("vector", lambda e: e.reduce_sum(
                    stat[:, 1:2], sq_scr[:, :], axis=mybir.AxisListType.X), inc="vv")
                W("vector", "vv", rs2)
                ts2 = EM("vector", lambda e: e.tensor_scalar(
                    stat[:, 2:3], stat[:, 1:2], 1.0 / D, EPS, OP.mult, OP.add), inc="vv")
                EM("vector", lambda e: e.tensor_mul(stat[:, 7:8], stat[:, 0:1], stat[:, 0:1]))
                W("vector", "vv", ts2)
                vv1 = EM("vector", lambda e: e.tensor_sub(stat[:, 2:3], stat[:, 2:3], stat[:, 7:8]), inc="vv")
                W("scalar", "vv", vv1)
                EM("scalar", lambda e: e.activation(stat[:, 3:4], stat[:, 2:3], AF.Ln))
                aa1 = EM("scalar", lambda e: e.activation(stat[:, 3:4], stat[:, 3:4], AF.Exp, scale=-0.5), inc="aa")
                W("vector", "aa", aa1)
                if t >= 2:
                    W("vector", "agp", spill_ack[t - 2])
                xnv = EM("vector", lambda e, b=b: e.tensor_scalar(
                    xn_t[b][:, :], rsd[b][:, :], stat[:, 0:1], stat[:, 3:4], OP.subtract, OP.mult), inc="vv")
                rsd_free[t] = xnv
                W("vector", "vv", xnv)
                sqv2 = EM("vector", lambda e, b=b: e.tensor_mul(sq_scr[:, :], xn_t[b][:, :], xn_t[b][:, :]),
                          inc="vv")
                W("vector", "vv", sqv2)
                rs3 = EM("vector", lambda e: e.reduce_sum(
                    stat[:, 4:5], sq_scr[:, :], axis=mybir.AxisListType.X), inc="vv")
                W("vector", "vv", rs3)
                vv2 = EM("vector", lambda e: e.tensor_scalar(
                    stat[:, 5:6], stat[:, 4:5], 1.0 / D, EPS, OP.mult, OP.add), inc="vv")
                W("scalar", "vv", vv2)
                EM("scalar", lambda e: e.activation(stat[:, 6:7], stat[:, 5:6], AF.Ln))
                aa2 = EM("scalar", lambda e: e.activation(stat[:, 6:7], stat[:, 6:7], AF.Exp, scale=-0.5), inc="aa")
                W("vector", "aa", aa2)
                rnv = EM("vector", lambda e, b=b: e.scalar_tensor_tensor(
                    rn_bf[b][:, :], xn_t[b][:, :], stat[:, 6:7], nwr[:, :], OP.mult, OP.mult), inc="vv")
                W("vector", "vv", rnv)
                vd = EM("vector", lambda e, b=b: e.tensor_copy(xn_bf[b][:, :], xn_t[b][:, :]), inc="vv")
                W("gpsimd", "vv", vd)
                EM("gpsimd", lambda e, b=b, t=t: e.dma_start(
                    out=XN_SP.ap()[t * 128:(t + 1) * 128, :], in_=xn_t[b][:, :]), inc="dgp", inck=16)
                EM("gpsimd", lambda e, b=b, t=t: e.dma_start(
                    out=ag_rn_in.ap()[t * 128:(t + 1) * 128, :], in_=rn_bf[b][:, :]), inc="dgp", inck=16)
                EM("gpsimd", lambda e, b=b, t=t: e.dma_start(
                    out=XNBF_SP.ap()[t * 128:(t + 1) * 128, :], in_=xn_bf[b][:, :]), inc="dgp", inck=16)
                spill_ack[t] = gack()

            cast_state = {"k": 0, "store": {}}

            def cast_chunks(src_ap, dst_dram, nrows, ncols, colchunk):
                for rt in range(nrows // 128):
                    for c0 in range(0, ncols, colchunk):
                        k = cast_state["k"]
                        cast_state["k"] += 1
                        kb = k % 2
                        q, qs, ak = ("gpsimd", "dgp", "agp") if kb == 0 else ("sync", "dsy", "asy")
                        EM(q, lambda e, rt=rt, c0=c0, cc=colchunk, kb=kb: e.dma_start(
                            out=wf32[:, kb * 2048:kb * 2048 + cc // 2],
                            in_=src_ap[rt * 128:(rt + 1) * 128, c0:c0 + cc // 2]), inc=qs, inck=16)
                        a_ld = p.ack(q, qs, ak)
                        W("vector", ak, a_ld)
                        if (k - 2) in cast_state["store"]:
                            sq2, sv2 = cast_state["store"][k - 2]
                            W("vector", sq2, sv2)
                        cv = EM("vector", lambda e, cc=colchunk, kb=kb: e.tensor_copy(
                            wbf[:, kb * 2048:kb * 2048 + cc // 2], wf32[:, kb * 2048:kb * 2048 + cc // 2]),
                            inc="vv")
                        W(q, "vv", cv)
                        EM(q, lambda e, rt=rt, c0=c0, cc=colchunk, kb=kb: e.dma_start(
                            out=dst_dram.ap()[rt * 128:(rt + 1) * 128, c0:c0 + cc // 2],
                            in_=wbf[:, kb * 2048:kb * 2048 + cc // 2]), inc=qs, inck=16)
                        cast_state["store"][k] = (ak, p.ack(q, qs, ak))
                        k = cast_state["k"]
                        cast_state["k"] += 1
                        kb = k % 2
                        q, qs, ak = ("gpsimd", "dgp", "agp") if kb == 0 else ("sync", "dsy", "asy")
                        EM(q, lambda e, rt=rt, c0=c0, cc=colchunk, kb=kb: e.dma_start(
                            out=wf32[:, kb * 2048:kb * 2048 + cc // 2],
                            in_=src_ap[rt * 128:(rt + 1) * 128, c0 + cc // 2:c0 + cc]), inc=qs, inck=16)
                        a_ld = p.ack(q, qs, ak)
                        W("vector", ak, a_ld)
                        if (k - 2) in cast_state["store"]:
                            sq2, sv2 = cast_state["store"][k - 2]
                            W("vector", sq2, sv2)
                        cv = EM("vector", lambda e, cc=colchunk, kb=kb: e.tensor_copy(
                            wbf[:, kb * 2048:kb * 2048 + cc // 2], wf32[:, kb * 2048:kb * 2048 + cc // 2]),
                            inc="vv")
                        W(q, "vv", cv)
                        EM(q, lambda e, rt=rt, c0=c0, cc=colchunk, kb=kb: e.dma_start(
                            out=dst_dram.ap()[rt * 128:(rt + 1) * 128, c0 + cc // 2:c0 + cc],
                            in_=wbf[:, kb * 2048:kb * 2048 + cc // 2]), inc=qs, inck=16)
                        cast_state["store"][k] = (ak, p.ack(q, qs, ak))

            cast_chunks(wguT_sl, ag_wgu_in, D // NCORE, 2 * I, 4096)
            cast_chunks(wdnT_sl, ag_wdn_in, I // NCORE, D, 2048)
            cast_chunks(owT_sl, ag_ow_in, E // NCORE, D, 2048)

            W("gpsimd", "dgp", p.c["dgp"])
            W("gpsimd", "asy", p.c["asy"])
            for iap, oap, sem in [
                (ag_rn_in, RN_ALL, "cc_rn"), (ag_wgu_in, WGUT, "cc_wgu"),
                (ag_wdn_in, WDNT, "cc_wdn"), (ag_ow_in, OWT, "cc_ow"),
            ]:
                EM("gpsimd", lambda e, iap=iap, oap=oap: e.collective_compute(
                    "AllGather", OP.bypass, replica_groups=[list(range(NCORE))],
                    ins=[iap.ap().opt()], outs=[oap.ap().opt()]), inc=sem)
            p.barrier(dma_flush=[("gpsimd", "dgp"), ("sync", "dsy")])

        with contextlib.ExitStack() as ph:
            sb = lambda name, shape, dtype=F32: ph.enter_context(nc.sbuf_tensor(name, shape, dtype))
            psb = lambda name, shape: ph.enter_context(nc.psum_tensor(name, shape, F32))
            wstage = sb("wstage", [128, 16 * 512])
            inw_bf = sb("inw_bf", [128, 16 * 512], BF16)
            skw_bf = sb("skw_bf", [128, 16 * 512], BF16)
            wd1_bf = sb("wd1_bf", [128, NET * 128], BF16)
            wbc_bf = sb("wbc_bf", [128, NET * 32], BF16)
            rhs = [sb(f"rhs{i}", [128, 16 * 512], BF16) for i in range(2)]
            xin_buf = [sb(f"xinb{et}", [128, 515]) for et in range(NET)]
            acc = [sb(f"acc{et}", [128, 512]) for et in range(NET)]
            sk2_t = [[sb(f"sk2t{i}_{et}", [128, 512]) for et in range(NET)] for i in range(2)]
            d1e = [sb(f"d1e{i}", [128, 512]) for i in range(2)]
            bce = [sb(f"bce{i}", [128, 512]) for i in range(2)]
            x_bf = sb("x_bf", [128, NET * L], BF16)
            ps_xin = [psb(f"ps_xin{i}", [128, 512]) for i in range(2)]
            ps_skip = [psb(f"ps_skip{i}", [128, 512]) for i in range(2)]
            ps_d1 = [psb(f"ps_d1{i}", [128, 512]) for i in range(2)]

            def load_cast(src, w, sl, nfree):
                EM("gpsimd", lambda e, src=src, sl=sl, nfree=nfree: e.dma_start(
                    out=wstage[:, 0:nfree].rearrange("p (t c) -> p t c", t=sl),
                    in_=src.rearrange("(t p) c -> p t c", p=128)), inc="dgp", inck=16)
                a_ld = gack()
                W("vector", "agp", a_ld)
                cv = EM("vector", lambda e, w=w, nfree=nfree: e.tensor_copy(w[:, 0:nfree], wstage[:, 0:nfree]),
                        inc="vv")
                W("gpsimd", "vv", cv)

            load_cast(inwT, inw_bf, 16, 16 * 512)
            load_cast(skwT, skw_bf, 16, 16 * 512)
            load_cast(wd1T, wd1_bf, NET, NET * 128)
            load_cast(wbcT, wbc_bf, NET, NET * 32)
            for et in range(NET - 1):
                EM("vector", lambda e, et=et: e.memset(xin_buf[et][:, 0:3], 0.0))
            v_w = EM("vector", lambda e: e.memset(xin_buf[NET - 1][:, 0:3], 0.0), inc="vv")

            W("sync", "cc_rn", 1)
            NLC = 16
            mm_xin, mm_skip, evac_xin, evac_skip = {}, {}, {}, {}
            xsilu, sk2_ack = {}, {}

            def rhs_load(lc):
                for dt in range(16):
                    EM("sync", lambda e, lc=lc, dt=dt: e.dma_start_transpose(
                        rhs[lc % 2][:, dt * 512:(dt + 1) * 512],
                        RN_ALL.ap()[lc * 512:(lc + 1) * 512, dt * 128:(dt + 1) * 128]),
                        inc="dsy", inck=16)

            rhs_load(0)
            for lc in range(NLC):
                b = lc % 2
                if lc + 1 < NLC:
                    if lc >= 1:
                        W("sync", "mm", mm_skip[(lc - 1, NET - 1)])
                    rhs_load(lc + 1)
                a_rhs = sack()
                W("tensor", "asy", a_rhs)
                if lc == 0:
                    W("tensor", "vv", v_w)
                for et in range(NET):
                    pb = et % 2
                    prev = (lc, et - 2) if et >= 2 else (lc - 1, et + 2)
                    if prev in evac_xin:
                        W("tensor", "vv", evac_xin[prev])
                    for dt in range(16):
                        mm_xin[(lc, et)] = EM("tensor", lambda e, et=et, dt=dt, b=b, pb=pb: e.matmul(
                            ps_xin[pb][:, :], inw_bf[:, dt * 512 + et * 128:dt * 512 + (et + 1) * 128],
                            rhs[b][:, dt * 512:(dt + 1) * 512], start=(dt == 0), stop=(dt == 15)),
                            inc="mm" if dt == 15 else None)
                    if prev in evac_skip:
                        W("tensor", "aa", evac_skip[prev])
                    for dt in range(16):
                        mm_skip[(lc, et)] = EM("tensor", lambda e, et=et, dt=dt, b=b, pb=pb: e.matmul(
                            ps_skip[pb][:, :], skw_bf[:, dt * 512 + et * 128:dt * 512 + (et + 1) * 128],
                            rhs[b][:, dt * 512:(dt + 1) * 512], start=(dt == 0), stop=(dt == 15)),
                            inc="mm" if dt == 15 else None)

                    W("vector", "mm", mm_xin[(lc, et)])
                    if (lc - 1, et) in xsilu:
                        W("vector", "aa", xsilu[(lc - 1, et)])
                    evv = EM("vector", lambda e, et=et, pb=pb: e.tensor_copy(
                        xin_buf[et][:, 3:515], ps_xin[pb][:, :]), inc="vv")
                    W("vector", "vv", evv)
                    cv0 = EM("vector", lambda e, et=et: e.tensor_scalar(
                        acc[et][:, :], xin_buf[et][:, 0:512], ecol(et, 0), None, OP.mult), inc="vv")
                    for k in range(1, 4):
                        W("vector", "vv", cv0)
                        cv0 = EM("vector", lambda e, et=et, k=k: e.scalar_tensor_tensor(
                            acc[et][:, :], xin_buf[et][:, k:k + 512], ecol(et, k),
                            acc[et][:, :], OP.mult, OP.add), inc="vv")
                    evac_xin[(lc, et)] = cv0
                    EM("vector", lambda e, et=et: e.tensor_copy(
                        xin_buf[et][:, 0:3], xin_buf[et][:, 512:515]), inc="vv")
                    W("scalar", "vv", evac_xin[(lc, et)])
                    xsilu[(lc, et)] = EM("scalar", lambda e, et=et, lc=lc: e.activation(
                        x_bf[:, et * L + lc * 512:et * L + (lc + 1) * 512], acc[et][:, :], AF.Silu,
                        bias=ecol(et, 4)), inc="aa")
                    W("scalar", "mm", mm_skip[(lc, et)])
                    if (lc - 2, et) in sk2_ack:
                        W("scalar", "agp", sk2_ack[(lc - 2, et)])
                    evac_skip[(lc, et)] = EM("scalar", lambda e, et=et, pb=pb, b=b: e.activation(
                        sk2_t[b][et][:, :], ps_skip[pb][:, :], AF.Silu), inc="aa")
                    W("gpsimd", "aa", evac_skip[(lc, et)])
                    EM("gpsimd", lambda e, et=et, lc=lc, b=b: e.dma_start(
                        out=SK2_SP.ap()[et * 128:(et + 1) * 128, lc * 512:(lc + 1) * 512],
                        in_=sk2_t[b][et][:, :]), inc="dgp", inck=16)
                    sk2_ack[(lc, et)] = gack()

            x_ready = p.c["aa"]

            d1_done, d1_ack = {}, {}
            W("tensor", "aa", x_ready)
            for lc in range(NLC):
                b = lc % 2
                if lc >= 2:
                    W("tensor", "vv", d1_done[lc - 2])
                m1 = m2 = None
                for et in range(NET):
                    m1 = EM("tensor", lambda e, et=et, lc=lc, b=b: e.matmul(
                        ps_d1[b][:, :], wd1_bf[:, et * 128:(et + 1) * 128],
                        x_bf[:, et * L + lc * 512:et * L + (lc + 1) * 512],
                        start=(et == 0), stop=(et == NET - 1)),
                        inc="mm" if et == NET - 1 else None)
                for et in range(NET):
                    m2 = EM("tensor", lambda e, et=et, lc=lc, b=b: e.matmul(
                        ps_skip[b][0:32, :], wbc_bf[:, et * 32:(et + 1) * 32],
                        x_bf[:, et * L + lc * 512:et * L + (lc + 1) * 512],
                        start=(et == 0), stop=(et == NET - 1)),
                        inc="mm" if et == NET - 1 else None)
                W("vector", "mm", m1)
                if lc >= 2:
                    W("vector", "agp", d1_ack[lc - 2])
                EM("vector", lambda e, b=b: e.tensor_copy(d1e[b][:, :], ps_d1[b][:, :]))
                W("vector", "mm", m2)
                d1_done[lc] = EM("vector", lambda e, b=b: e.tensor_copy(bce[b][0:32, :], ps_skip[b][0:32, :]),
                                 inc="vv")
                W("gpsimd", "vv", d1_done[lc])
                EM("gpsimd", lambda e, lc=lc, b=b: e.dma_start(
                    out=ar_in.ap()[0:R, lc * 512:(lc + 1) * 512], in_=d1e[b][:, :]), inc="dgp", inck=16)
                EM("gpsimd", lambda e, lc=lc, b=b: e.dma_start(
                    out=ar_in.ap()[R:R + 32, lc * 512:(lc + 1) * 512], in_=bce[b][0:32, :]), inc="dgp", inck=16)
                d1_ack[lc] = gack()
            W("gpsimd", "dgp", p.c["dgp"])
            EM("gpsimd", lambda e: e.collective_compute(
                "AllReduce", OP.add, replica_groups=[list(range(NCORE))],
                ins=[ar_in.ap().opt()], outs=[ar_out.ap().opt()]), inc="cc_ar")
            W("gpsimd", "aa", x_ready)
            for et in range(NET):
                EM("gpsimd", lambda e, et=et: e.dma_start(
                    out=X_SP.ap()[et * 128:(et + 1) * 128, :], in_=x_bf[:, et * L:(et + 1) * L]),
                    inc="dgp", inck=16)
            p.barrier(dma_flush=[("gpsimd", "dgp"), ("sync", "dsy")])

        with contextlib.ExitStack() as ph:
            sb = lambda name, shape, dtype=F32: ph.enter_context(nc.sbuf_tensor(name, shape, dtype))
            psb = lambda name, shape: ph.enter_context(nc.psum_tensor(name, shape, F32))
            d1t_f = sb("d1t_f", [128, CH])
            d1t_bf = sb("d1t_bf", [128, CH], BF16)
            bmc_bf = sb("bmc_bf", [32, L], BF16)
            wd2_bf = sb("wd2_bf", [128, EC], BF16)
            delta = sb("delta", [128, NET * CH])
            xc_bf = sb("xc_bf", [128, NET * CH], BF16)
            u = sb("u", [128, NET * CH])
            y_acc = sb("y_acc", [128, NET * CH])
            sk2c = [sb(f"sk2c{i}", [128, CH]) for i in range(2)]
            p_buf = sb("p_buf", [128, 2 * CH])
            ug_buf = sb("ug_buf", [128, 2 * CH])
            h_buf = sb("h_buf", [128, 2 * CH])
            prod = sb("prod", [128, CH])
            brep = sb("brep", [128, 2 * CH])
            crep = sb("crep", [128, 2 * CH])
            t0 = sb("t0", [128, CH])
            t1 = sb("t1", [128, CH])
            y_bf = [sb(f"y_bf{i}", [128, CH], BF16) for i in range(2)]
            sel_bf = sb("sel_bf", [32, 32 * 128], BF16)
            ps_dp = psb("ps_dp", [128, CH])
            ps_b = psb("ps_b", [128, CH])
            ps_c = psb("ps_c", [128, CH])

            EM("gpsimd", lambda e: e.dma_start(out=sel_bf[:, :], in_=sel_in), inc="dgp", inck=16)
            EM("gpsimd", lambda e: e.dma_start(out=d1t_f[:, 0:EC], in_=wd2T), inc="dgp", inck=16)
            a_w = gack()
            W("vector", "agp", a_w)
            wcv = EM("vector", lambda e: e.tensor_copy(wd2_bf[:, :], d1t_f[:, 0:EC]), inc="vv")
            W("sync", "cc_ar", 1)
            W("sync", "vv", wcv)
            for cb in range(NCH):
                EM("sync", lambda e, cb=cb: e.dma_start(
                    out=t0[0:32, 0:CH], in_=ar_out.ap()[R:R + 32, cb * CH:(cb + 1) * CH]), inc="dsy", inck=16)
                a_b = sack()
                W("vector", "asy", a_b)
                bv = EM("vector", lambda e, cb=cb: e.tensor_copy(
                    bmc_bf[:, cb * CH:(cb + 1) * CH], t0[0:32, 0:CH]), inc="vv")
                W("sync", "vv", bv)
            W("tensor", "agp", a_w)

            scan_done, brep_ok, pg_done = {}, {}, {}
            y_ack, fixv = {}, {}
            sk_ack = {}
            delta_ok = {}
            for cnk in range(NCH):
                c0 = cnk * CH
                if cnk >= 1:
                    W("sync", "vv", delta_ok[(cnk - 1, 0)])
                EM("sync", lambda e, c0=c0: e.dma_start(
                    out=d1t_f[:, :], in_=ar_out.ap()[0:R, c0:c0 + CH]), inc="dsy", inck=16)
                a_d1 = sack()
                W("vector", "asy", a_d1)
                dcv = EM("vector", lambda e: e.tensor_copy(d1t_bf[:, :], d1t_f[:, :]), inc="vv")
                W("tensor", "vv", dcv)
                for et in range(NET):
                    mdp = None
                    for hh in range(2):
                        mdp = EM("tensor", lambda e, et=et, hh=hh: e.matmul(
                            ps_dp[:, hh * 512:(hh + 1) * 512], wd2_bf[:, et * 128:(et + 1) * 128],
                            d1t_bf[:, hh * 512:(hh + 1) * 512], start=True, stop=True),
                            inc="mm" if hh == 1 else None)
                    W("scalar", "mm", mdp)
                    if (cnk, et - 1) in delta_ok:
                        W("scalar", "vv", delta_ok[(cnk, et - 1)])
                    elif cnk >= 1:
                        W("scalar", "vv", fixv[(cnk - 1, NET - 1)])
                    EM("scalar", lambda e, et=et: e.activation(t0[:, :], ps_dp[:, :], AF.Relu, bias=ecol(et, 5)))
                    EM("scalar", lambda e, et=et: e.activation(t1[:, :], ps_dp[:, :], AF.Abs, bias=ecol(et, 5)))
                    EM("scalar", lambda e: e.activation(t1[:, :], t1[:, :], AF.Exp, scale=-1.0))
                    sdn = EM("scalar", lambda e: e.activation(t1[:, :], t1[:, :], AF.Ln, bias=1.0), inc="aa")
                    W("tensor", "aa", sdn)
                    W("vector", "aa", sdn)
                    delta_ok[(cnk, et)] = EM("vector", lambda e, et=et: e.tensor_add(
                        delta[:, et * CH:(et + 1) * CH], t0[:, :], t1[:, :]), inc="vv")
                if cnk >= 1:
                    W("gpsimd", "vv", fixv[(cnk - 1, NET - 1)])
                EM("gpsimd", lambda e, c0=c0: e.dma_start(
                    out=xc_bf[:, :].rearrange("p (et l) -> p et l", et=NET),
                    in_=X_SP.ap()[:, c0:c0 + CH].rearrange("(et p) l -> p et l", p=128)),
                    inc="dgp", inck=16)
                a_x = gack()
                if cnk >= 1:
                    W("vector", "gg", pg_done[(cnk - 1, N - 1, NET - 1)])
                W("vector", "agp", a_x)
                for et in range(NET):
                    EM("vector", lambda e, et=et: e.tensor_copy(t1[:, :], xc_bf[:, et * CH:(et + 1) * CH]))
                    EM("vector", lambda e, et=et: e.tensor_mul(
                        u[:, et * CH:(et + 1) * CH], delta[:, et * CH:(et + 1) * CH], t1[:, :]))
                uv = EM("vector", lambda e: e.memset(y_acc[:, :], 0.0), inc="vv")

                for n in range(N):
                    nb = n % 2
                    if (cnk, n - 1) in brep_ok:
                        W("tensor", "aa", brep_ok[(cnk, n - 1)])
                    elif cnk >= 1 and n == 0:
                        W("tensor", "aa", brep_ok[(cnk - 1, N - 1)])
                    for hh in range(2):
                        EM("tensor", lambda e, n=n, hh=hh, c0=c0: e.matmul(
                            ps_b[:, hh * 512:(hh + 1) * 512], sel_bf[0:32, n * 128:(n + 1) * 128],
                            bmc_bf[0:32, c0 + hh * 512:c0 + (hh + 1) * 512], start=True, stop=True))
                    mbc = None
                    for hh in range(2):
                        mbc = EM("tensor", lambda e, n=n, hh=hh, c0=c0: e.matmul(
                            ps_c[:, hh * 512:(hh + 1) * 512], sel_bf[0:32, (16 + n) * 128:(17 + n) * 128],
                            bmc_bf[0:32, c0 + hh * 512:c0 + (hh + 1) * 512], start=True, stop=True),
                            inc="mm" if hh == 1 else None)
                    W("scalar", "mm", mbc)
                    key = (cnk, n - 2, NET - 1) if n >= 2 else ((cnk - 1, N - 2 + n, NET - 1) if cnk >= 1 else None)
                    if key in pg_done:
                        W("scalar", "gg", pg_done[key])
                    skey = (cnk, (n - 2) * NET + NET - 1) if n >= 2 else (
                        (cnk - 1, (N - 2 + n) * NET + NET - 1) if cnk >= 1 else None)
                    if skey in scan_done:
                        W("scalar", "vv", scan_done[skey])
                    EM("scalar", lambda e, nb=nb: e.activation(brep[:, nb * CH:(nb + 1) * CH], ps_b[:, :],
                                                               AF.Identity))
                    bcv = EM("scalar", lambda e, nb=nb: e.activation(crep[:, nb * CH:(nb + 1) * CH], ps_c[:, :],
                                                                     AF.Identity), inc="aa")
                    brep_ok[(cnk, n)] = bcv

                    for et in range(NET):
                        s = n * NET + et
                        sb_ = s % 2
                        pkey = (cnk, s - 2) if s >= 2 else ((cnk - 1, N * NET - 2 + s) if cnk >= 1 else None)
                        W("scalar", "vv", uv)
                        if pkey in scan_done:
                            W("scalar", "vv", scan_done[pkey])
                        pa = EM("scalar", lambda e, et=et, n=n, sb_=sb_: e.activation(
                            p_buf[:, sb_ * CH:(sb_ + 1) * CH], delta[:, et * CH:(et + 1) * CH], AF.Exp,
                            scale=asb[:, et * 16 + n:et * 16 + n + 1]), inc="aa")
                        W("gpsimd", "aa", bcv)
                        W("gpsimd", "vv", uv)
                        if pkey in scan_done:
                            W("gpsimd", "vv", scan_done[pkey])
                        pg = EM("gpsimd", lambda e, et=et, nb=nb, sb_=sb_: e.tensor_mul(
                            ug_buf[:, sb_ * CH:(sb_ + 1) * CH], u[:, et * CH:(et + 1) * CH],
                            brep[:, nb * CH:(nb + 1) * CH]), inc="gg")
                        pg_done[(cnk, n, et)] = pg
                        W("vector", "aa", pa)
                        W("vector", "gg", pg)
                        cidx = et * N + n
                        if cnk == 0:
                            sv0 = EM("vector", lambda e, sb_=sb_: e.tensor_tensor_scan(
                                h_buf[:, sb_ * CH:(sb_ + 1) * CH], p_buf[:, sb_ * CH:(sb_ + 1) * CH],
                                ug_buf[:, sb_ * CH:(sb_ + 1) * CH], 0.0, OP.mult, OP.add), inc="vv")
                        else:
                            sv0 = EM("vector", lambda e, sb_=sb_, cidx=cidx: e.tensor_tensor_scan(
                                h_buf[:, sb_ * CH:(sb_ + 1) * CH], p_buf[:, sb_ * CH:(sb_ + 1) * CH],
                                ug_buf[:, sb_ * CH:(sb_ + 1) * CH], carry[:, cidx:cidx + 1], OP.mult, OP.add),
                                inc="vv")
                        W("vector", "vv", sv0)
                        if debug and cnk == 0 and n == 0 and et == 0:
                            W("gpsimd", "vv", sv0)
                            for j, src_ap in enumerate([
                                    lambda: p_buf[:, 0:CH], lambda: ug_buf[:, 0:CH],
                                    lambda: h_buf[:, 0:CH], lambda: u[:, 0:CH],
                                    lambda: brep[:, 0:CH]]):
                                EM("gpsimd", lambda e, j=j, src_ap=src_ap: e.dma_start(
                                    out=dbg["dbg_scan"][:, j * CH:(j + 1) * CH], in_=src_ap()),
                                    inc="dgp", inck=16)
                            EM("gpsimd", lambda e: e.dma_start(
                                out=dbg["dbg_scan"][:, 5 * CH:6 * CH], in_=crep[:, 0:CH]),
                                inc="dgp", inck=16)
                            scn_a = gack()
                            W("scalar", "agp", scn_a)
                            W("vector", "agp", scn_a)
                            W("gpsimd", "agp", scn_a)
                        EM("vector", lambda e, sb_=sb_, cidx=cidx: e.tensor_copy(
                            carry[:, cidx:cidx + 1], h_buf[:, sb_ * CH + CH - 1:sb_ * CH + CH]))
                        EM("vector", lambda e, sb_=sb_, nb=nb: e.tensor_mul(
                            prod[:, :], h_buf[:, sb_ * CH:(sb_ + 1) * CH], crep[:, nb * CH:(nb + 1) * CH]))
                        scan_done[(cnk, s)] = EM("vector", lambda e, et=et: e.tensor_add(
                            y_acc[:, et * CH:(et + 1) * CH], y_acc[:, et * CH:(et + 1) * CH], prod[:, :]),
                            inc="vv")

                for et in range(NET):
                    yb = et % 2
                    skkey = (cnk, et - 2) if et >= 2 else ((cnk - 1, et + 2) if cnk >= 1 else None)
                    if skkey in fixv:
                        W("gpsimd", "vv", fixv[skkey])
                    EM("gpsimd", lambda e, et=et, c0=c0, yb=yb: e.dma_start(
                        out=sk2c[yb][:, :], in_=SK2_SP.ap()[et * 128:(et + 1) * 128, c0:c0 + CH]),
                        inc="dgp", inck=16)
                    sk_ack[(cnk, et)] = gack()
                    t1v = EM("vector", lambda e, et=et: e.tensor_copy(
                        t1[:, :], xc_bf[:, et * CH:(et + 1) * CH]), inc="vv")
                    W("vector", "vv", t1v)
                    fx0 = EM("vector", lambda e, et=et: e.scalar_tensor_tensor(
                        y_acc[:, et * CH:(et + 1) * CH], t1[:, :], ecol(et, 6),
                        y_acc[:, et * CH:(et + 1) * CH], OP.mult, OP.add), inc="vv")
                    W("vector", "vv", fx0)
                    W("vector", "agp", sk_ack[(cnk, et)])
                    ykey = (cnk, et - 2) if et >= 2 else ((cnk - 1, et + 2) if cnk >= 1 else None)
                    if ykey in y_ack:
                        W("vector", "agp", y_ack[ykey])
                    fixv[(cnk, et)] = EM("vector", lambda e, et=et, yb=yb: e.tensor_mul(
                        y_bf[yb][:, :], y_acc[:, et * CH:(et + 1) * CH], sk2c[yb][:, :]), inc="vv")
                    W("gpsimd", "vv", fixv[(cnk, et)])
                    EM("gpsimd", lambda e, et=et, c0=c0, yb=yb: e.dma_start(
                        out=ag_y_in.ap()[et * 128:(et + 1) * 128, c0:c0 + CH], in_=y_bf[yb][:, :]),
                        inc="dgp", inck=16)
                    y_ack[(cnk, et)] = gack()
                if debug:
                    W("gpsimd", "vv", p.c["vv"])
                    for et in range(NET):
                        EM("gpsimd", lambda e, et=et, c0=c0: e.dma_start(
                            out=dbg["dbg_delta"][et * 128:(et + 1) * 128, c0:c0 + CH],
                            in_=delta[:, et * CH:(et + 1) * CH]), inc="dgp", inck=16)
                    dbg_a = gack()
                    W("vector", "agp", dbg_a)

            W("gpsimd", "dgp", p.c["dgp"])
            EM("gpsimd", lambda e: e.collective_compute(
                "AllGather", OP.bypass, replica_groups=[list(range(NCORE))],
                ins=[ag_y_in.ap().opt()], outs=[Y_ALL.ap().opt()]), inc="cc_y")
            p.barrier(dma_flush=[("gpsimd", "dgp"), ("sync", "dsy")])

        with contextlib.ExitStack() as ph:
            sb = lambda name, shape, dtype=F32: ph.enter_context(nc.sbuf_tensor(name, shape, dtype))
            psb = lambda name, shape: ph.enter_context(nc.psum_tensor(name, shape, F32))
            xnt = sb("xnt", [128, 16 * 512], BF16)
            wg_g = [sb(f"wg_g{i}", [128, 16 * 128], BF16) for i in range(2)]
            wg_u = [sb(f"wg_u{i}", [128, 16 * 128], BF16) for i in range(2)]
            sg = [sb(f"sg{i}", [128, 512]) for i in range(2)]
            h_bf = sb("h_bf", [128, 64 * 512], BF16)
            wdn_t = [sb(f"wdn_t{i}", [128, 512], BF16) for i in range(2)]
            o2_t = sb("o2_t", [128, 512])
            ps_g = [psb(f"ps_g{i}", [128, 512]) for i in range(2)]
            ps_u = [psb(f"ps_u{i}", [128, 512]) for i in range(2)]
            ps_dn = [psb(f"ps_dn{i}", [128, 512]) for i in range(4)]

            W("sync", "cc_wgu", 1)
            W("sync", "cc_wdn", 1)
            gu_mm, h_done, dn_mm, dn_evac = {}, {}, {}, {}
            o2_ack = {}
            for lh in range(2):
                l0 = lh * 512
                if lh == 1:
                    W("sync", "mm", gu_mm[(0, 63)])
                for dt in range(16):
                    EM("sync", lambda e, dt=dt, l0=l0: e.dma_start_transpose(
                        xnt[:, dt * 512:(dt + 1) * 512],
                        XNBF_SP.ap()[l0:l0 + 512, dt * 128:(dt + 1) * 128]), inc="dsy", inck=16)
                a_xnt = sack()
                for it in range(64):
                    b = it % 2
                    q = "sync" if b == 0 else "gpsimd"
                    qs, ak = ("dsy", "asy") if b == 0 else ("dgp", "agp")
                    if it >= 2:
                        W(q, "mm", gu_mm[(lh, it - 2)])
                    EM(q, lambda e, it=it, b=b: e.dma_start(
                        out=wg_g[b][:, :].rearrange("p (dt i) -> p dt i", dt=16),
                        in_=WGUT.ap()[:, it * 128:(it + 1) * 128].rearrange("(dt p) i -> p dt i", p=128)),
                        inc=qs, inck=16)
                    EM(q, lambda e, it=it, b=b: e.dma_start(
                        out=wg_u[b][:, :].rearrange("p (dt i) -> p dt i", dt=16),
                        in_=WGUT.ap()[:, I + it * 128:I + (it + 1) * 128].rearrange("(dt p) i -> p dt i", p=128)),
                        inc=qs, inck=16)
                    a_wg = p.ack(q, qs, ak)
                    W("tensor", ak, a_wg)
                    W("tensor", "asy", a_xnt)
                    if (lh, it - 2) in h_done:
                        W("tensor", "vv", h_done[(lh, it - 2)])
                    elif lh == 1 and it < 2:
                        W("tensor", "vv", h_done[(0, 62 + it)])
                    for dt in range(16):
                        EM("tensor", lambda e, dt=dt, b=b: e.matmul(
                            ps_g[b][:, :], wg_g[b][:, dt * 128:(dt + 1) * 128], xnt[:, dt * 512:(dt + 1) * 512],
                            start=(dt == 0), stop=(dt == 15)))
                    for dt in range(16):
                        gu_mm[(lh, it)] = EM("tensor", lambda e, dt=dt, b=b: e.matmul(
                            ps_u[b][:, :], wg_u[b][:, dt * 128:(dt + 1) * 128], xnt[:, dt * 512:(dt + 1) * 512],
                            start=(dt == 0), stop=(dt == 15)), inc="mm" if dt == 15 else None)
                    W("scalar", "mm", gu_mm[(lh, it)])
                    if (lh, it - 2) in h_done:
                        W("scalar", "vv", h_done[(lh, it - 2)])
                    elif lh == 1 and it < 2:
                        W("scalar", "vv", h_done[(0, 62 + it)])
                    sa = EM("scalar", lambda e, b=b: e.activation(sg[b][:, :], ps_g[b][:, :], AF.Silu), inc="aa")
                    W("vector", "aa", sa)
                    if lh == 1 and it == 0:
                        W("vector", "mm", dn_mm[(0, 3, 63)])
                    h_done[(lh, it)] = EM("vector", lambda e, it=it, b=b: e.tensor_mul(
                        h_bf[:, it * 512:(it + 1) * 512], sg[b][:, :], ps_u[b][:, :]), inc="vv")
                W("tensor", "vv", h_done[(lh, 63)])
                for dc in range(4):
                    if (lh, dc - 1) in dn_evac:
                        W("tensor", "vv", dn_evac[(lh, dc - 1)])
                    elif lh == 1 and dc == 0:
                        W("tensor", "vv", dn_evac[(0, 3)])
                    for it in range(64):
                        b = it % 2
                        q = "sync" if b == 0 else "gpsimd"
                        qs, ak = ("dsy", "asy") if b == 0 else ("dgp", "agp")
                        if it >= 2:
                            W(q, "mm", dn_mm[(lh, dc, it - 2)])
                        elif dc > 0:
                            W(q, "mm", dn_mm[(lh, dc - 1, 62 + it)])
                        EM(q, lambda e, it=it, dc=dc, b=b: e.dma_start(
                            out=wdn_t[b][:, :], in_=WDNT.ap()[it * 128:(it + 1) * 128, dc * 512:(dc + 1) * 512]),
                            inc=qs, inck=16)
                        a_wdn = p.ack(q, qs, ak)
                        W("tensor", ak, a_wdn)
                        for lt2 in range(4):
                            dn_mm[(lh, dc, it)] = EM("tensor", lambda e, it=it, b=b, lt2=lt2: e.matmul(
                                ps_dn[lt2][:, :], h_bf[:, it * 512 + lt2 * 128:it * 512 + (lt2 + 1) * 128],
                                wdn_t[b][:, :], start=(it == 0), stop=(it == 63)),
                                inc="mm" if lt2 == 3 else None)
                    W("vector", "mm", dn_mm[(lh, dc, 63)])
                    ev = None
                    for lt2 in range(4):
                        prev = o2_ack.get((lh, dc, lt2 - 1)) or o2_ack.get((lh, dc - 1, 3)) or \
                            o2_ack.get((lh - 1, 3, 3))
                        if prev:
                            W("vector", "agp", prev)
                        ev = EM("vector", lambda e, lt2=lt2: e.tensor_copy(o2_t[:, :], ps_dn[lt2][:, :]),
                                inc="vv")
                        W("gpsimd", "vv", ev)
                        EM("gpsimd", lambda e, lh=lh, lt2=lt2, dc=dc: e.dma_start(
                            out=OUT2_SP.ap()[lh * 512 + lt2 * 128:lh * 512 + (lt2 + 1) * 128,
                                             dc * 512:(dc + 1) * 512], in_=o2_t[:, :]), inc="dgp", inck=16)
                        o2_ack[(lh, dc, lt2)] = gack()
                    dn_evac[(lh, dc)] = ev
            p.barrier(dma_flush=[("gpsimd", "dgp"), ("sync", "dsy")])

        with contextlib.ExitStack() as ph:
            sb = lambda name, shape, dtype=F32: ph.enter_context(nc.sbuf_tensor(name, shape, dtype))
            psb = lambda name, shape: ph.enter_context(nc.psum_tensor(name, shape, F32))
            y_sb = sb("y_sb", [128, 32 * LC], BF16)
            owt_t = [sb(f"owt_t{i}", [128, 512], BF16) for i in range(2)]
            xn_t4 = sb("xn_t4", [128, 512])
            o2_t4 = sb("o2_t4", [128, 512])
            g_t4 = sb("g_t4", [128, 512])
            rs_t4 = sb("rs_t4", [128, 512])
            s_t4 = sb("s_t4", [128, 512])
            out_t4 = sb("out_t4", [128, 512])
            ps_o = [psb(f"ps_o{i}", [128, 512]) for i in range(8)]

            W("sync", "cc_y", 1)
            W("sync", "cc_ow", 1)

            def y_load(e):
                for arm in e.switch_core_id(n=128):
                    lcore = physical_to_logical_core_id(int(arm) % 128, 0, container_num_cores=NCORE) % NCORE
                    c0 = lcore * LC
                    e.dma_start(
                        out=y_sb[:, :].rearrange("p (et l) -> p et l", et=32),
                        in_=Y_ALL.ap()[:, c0:c0 + LC].rearrange("(et p) l -> p et l", p=128),
                    ).then_inc(p.sems["dsy"], 16)
            p.c["dsy"] += 16
            p.prog["sync"].append(y_load)
            yl_ack = sack()

            ow_mm, ph3_evac, od_ack = {}, {}, {}
            for dc in range(4):
                for et8 in range(32):
                    b = et8 % 2
                    qw = "sync" if b == 0 else "gpsimd"
                    if (dc, et8 - 2) in ow_mm:
                        W(qw, "mm", ow_mm[(dc, et8 - 2)])
                    elif dc > 0 and et8 < 2:
                        W(qw, "mm", ow_mm[(dc - 1, 30 + et8)])
                    q, qs, ak = ("sync", "dsy", "asy") if b == 0 else ("gpsimd", "dgp", "agp")
                    EM(q, lambda e, et8=et8, dc=dc, b=b: e.dma_start(
                        out=owt_t[b][:, :], in_=OWT.ap()[et8 * 128:(et8 + 1) * 128, dc * 512:(dc + 1) * 512]),
                        inc=qs, inck=16)
                    a_ow = p.ack(q, qs, ak)
                    W("tensor", ak, a_ow)
                    W("tensor", "asy", yl_ack)
                    if dc > 0 and et8 == 0:
                        W("tensor", "vv", ph3_evac[dc - 1])
                    for lt in range(8):
                        ow_mm[(dc, et8)] = EM("tensor", lambda e, et8=et8, b=b, lt=lt: e.matmul(
                            ps_o[lt][:, :], y_sb[:, et8 * LC + lt * 128:et8 * LC + (lt + 1) * 128],
                            owt_t[b][:, :], start=(et8 == 0), stop=(et8 == 31)),
                            inc="mm" if lt == 7 else None)
                last_ov = None
                for lt in range(8):
                    prev_ov = last_ov if lt > 0 else (ph3_evac.get(dc - 1) if dc > 0 else None)
                    if prev_ov:
                        W("gpsimd", "vv", prev_ov)
                    W("gpsimd", "mm", ow_mm[(dc, 31)])
                    EM("gpsimd", lambda e, lt=lt, dc=dc: e.dma_start(
                        out=xn_t4[:, :], in_=XN_SP.ap()[lt * 128:(lt + 1) * 128, dc * 512:(dc + 1) * 512]),
                        inc="dgp", inck=16)
                    EM("gpsimd", lambda e, lt=lt, dc=dc: e.dma_start(
                        out=o2_t4[:, :], in_=OUT2_SP.ap()[lt * 128:(lt + 1) * 128, dc * 512:(dc + 1) * 512]),
                        inc="dgp", inck=16)
                    EM("gpsimd", lambda e, lt=lt, dc=dc: e.dma_start(
                        out=rs_t4[:, :], in_=resid[lt * 128:(lt + 1) * 128, dc * 512:(dc + 1) * 512]),
                        inc="dgp", inck=16)
                    a_ld3 = gack()
                    W("scalar", "agp", a_ld3)
                    sg4 = EM("scalar", lambda e: e.activation(g_t4[:, :], o2_t4[:, :], AF.Silu), inc="aa")
                    W("vector", "agp", a_ld3)
                    W("vector", "aa", sg4)
                    W("vector", "mm", ow_mm[(dc, 31)])
                    prev_od = od_ack.get((dc, lt - 1)) or (od_ack.get((dc - 1, 7)) if dc > 0 else None)
                    if prev_od:
                        W("vector", "agp", prev_od)
                    EM("vector", lambda e, lt=lt: e.tensor_add(s_t4[:, :], xn_t4[:, :], ps_o[lt][:, :]))
                    EM("vector", lambda e: e.tensor_mul(s_t4[:, :], s_t4[:, :], g_t4[:, :]))
                    last_ov = EM("vector", lambda e: e.tensor_add(out_t4[:, :], rs_t4[:, :], s_t4[:, :]), inc="vv")
                    W("gpsimd", "vv", last_ov)
                    EM("gpsimd", lambda e, lt=lt, dc=dc: e.dma_start(
                        out=out[lt * 128:(lt + 1) * 128, dc * 512:(dc + 1) * 512], in_=out_t4[:, :]),
                        inc="dgp", inck=16)
                    od_ack[(dc, lt)] = gack()
                ph3_evac[dc] = last_ov

            if debug:
                for src, dst in [(XN_SP, dbg["dbg_xn"]), (ag_rn_in, dbg["dbg_rn"]),
                                 (ar_out, dbg["dbg_d1bc"]), (SK2_SP, dbg["dbg_sk2"]),
                                 (ag_y_in, dbg["dbg_y"]), (OUT2_SP, dbg["dbg_out2"]),
                                 (X_SP, dbg["dbg_x"])]:
                    EM("gpsimd", lambda e, src=src, dst=dst: e.dma_start(out=dst, in_=src.ap()),
                       inc="dgp", inck=16)
            W("gpsimd", "dgp", p.c["dgp"])
            p.barrier(dma_flush=[("gpsimd", "dgp"), ("sync", "dsy")])

        return _finish(nc, p)


def _finish(nc, p):
    with nc.Block() as block:
        @block.sync
        def _(e):
            for fn in p.prog["sync"]:
                fn(e)

        @block.gpsimd
        def _(e):
            for fn in p.prog["gpsimd"]:
                fn(e)

        @block.vector
        def _(e):
            for fn in p.prog["vector"]:
                fn(e)

        @block.scalar
        def _(e):
            for fn in p.prog["scalar"]:
                fn(e)

        @block.tensor
        def _(e):
            for fn in p.prog["tensor"]:
                fn(e)
    return nc


def make_in_maps(inputs):
    x0 = np.asarray(inputs["L0_attn_resid"]).reshape(L, D)
    norm_w = np.asarray(inputs["L0_attn_norm_w"])
    skip_w = np.asarray(inputs["L0_attn_skip_w"])
    in_w = np.asarray(inputs["L0_attn_in_w"])
    conv_w = np.asarray(inputs["L0_attn_conv_w"])
    conv_b = np.asarray(inputs["L0_attn_conv_b"])
    wd1 = np.asarray(inputs["L0_attn_wd1"])
    wd2 = np.asarray(inputs["L0_attn_wd2"])
    wd2_b = np.asarray(inputs["L0_attn_wd2_b"])
    wB = np.asarray(inputs["L0_attn_wB"])
    wC = np.asarray(inputs["L0_attn_wC"])
    A_log = np.asarray(inputs["L0_attn_A_log"])
    W_D = np.asarray(inputs["L0_attn_W_D"])
    out_w = np.asarray(inputs["L0_attn_out_w"])
    wgu = np.asarray(inputs["L0_ffn_w_gate_up"])
    wdn = np.asarray(inputs["L0_ffn_w_down"])

    nw_rep = np.ascontiguousarray(np.broadcast_to(norm_w[None, :], (128, D)).astype(np.float32))
    wguT = np.ascontiguousarray(wgu.T)
    wdnT = np.ascontiguousarray(wdn.T)
    owT = np.ascontiguousarray(out_w.T)
    sel = np.zeros((32, 32 * 128), np.float32)
    for k in range(32):
        sel[k, k * 128:(k + 1) * 128] = 1.0
    sel_bf = sel.astype(ml_dtypes.bfloat16)

    in_maps = []
    for i in range(NCORE):
        ls = slice(i * LC, (i + 1) * LC)
        es = slice(i * EC, (i + 1) * EC)
        ev = np.zeros((EC, 24), np.float32)
        ev[:, 0:4] = conv_w[es]
        ev[:, 4] = conv_b[es]
        ev[:, 5] = wd2_b[es]
        ev[:, 6] = W_D[es]
        ev[:, 8:24] = A_log[es]
        in_maps.append({
            "resid": np.ascontiguousarray(x0[ls]),
            "nw_rep": nw_rep,
            "inwT": np.ascontiguousarray(in_w[es].T),
            "skwT": np.ascontiguousarray(skip_w[es].T),
            "wd1T": np.ascontiguousarray(wd1[:, es].T),
            "wd2T": np.ascontiguousarray(wd2[es].T),
            "wbcT": np.ascontiguousarray(np.concatenate([wB[:, es].T, wC[:, es].T], axis=1)),
            "evec": ev,
            "sel_in": sel_bf,
            "wguT_sl": np.ascontiguousarray(wguT[i * (D // NCORE):(i + 1) * (D // NCORE)]),
            "wdnT_sl": np.ascontiguousarray(wdnT[i * (I // NCORE):(i + 1) * (I // NCORE)]),
            "owT_sl": np.ascontiguousarray(owT[i * (E // NCORE):(i + 1) * (E // NCORE)]),
        })
    return in_maps


def kernel(**inputs):
    in_maps = make_in_maps(inputs)
    nc = build_nc()
    res = run_bass_kernel_spmd(nc, in_maps, core_ids=list(range(NCORE)))
    outp = np.concatenate([res.results[i]["out"] for i in range(NCORE)], axis=0)
    return outp.reshape(1, L, D).astype(np.float32)


# version 03
# speedup vs baseline: 1.0617x; 1.0266x over previous
import contextlib
import numpy as np
import ml_dtypes
import concourse.bass as bass
import concourse.mybir as mybir
from concourse.bass_utils import run_bass_kernel_spmd
from concourse.libnrt import physical_to_logical_core_id

F32 = mybir.dt.float32
BF16 = mybir.dt.bfloat16
AF = mybir.ActivationFunctionType
OP = mybir.AluOpType

NCORE = 8
L = 8192
D = 2048
E = 4096
N = 16
R = 128
I = 8192
LC = L // NCORE
EC = E // NCORE
NET = EC // 128
CH = 1024
NCH = L // CH
EPS = 1e-5

ENGINES = ("sync", "gpsimd", "vector", "scalar", "tensor")


class P:

    def __init__(self, nc, stack, semnames):
        self.nc = nc
        self.prog = {e: [] for e in ENGINES}
        self.sems = {s: stack.enter_context(nc.semaphore(s)) for s in semnames}
        self.c = {s: 0 for s in semnames}
        self._last = {}
        self.nbar = 0

    def wait(self, eng, sem, val):
        if val is None or val <= 0:
            return
        self.prog[eng].append(lambda e, s=sem, v=val: e.wait_ge(self.sems[s], v))

    CHAIN = {"vector": "vv", "scalar": "aa", "gpsimd": "gg"}

    def emit(self, eng, builder, inc=None, inck=1):
        chain_sem = self.CHAIN.get(eng)
        dma_like = inc in ("dgp", "dsy") or (inc or "").startswith("cc")
        if chain_sem is not None and not dma_like:
            self.wait(eng, chain_sem, self._last.get(eng, 0))
            if inc is None:
                inc, inck = chain_sem, 1
            assert inc == chain_sem, (eng, inc)
        if inc is not None:
            self.c[inc] += inck
            v = self.c[inc]

            def fn(e, b=builder, s=inc, k=inck):
                b(e).then_inc(self.sems[s], k)
            self.prog[eng].append(fn)
            if chain_sem is not None and not dma_like:
                self._last[eng] = v
            return v
        self.prog[eng].append(lambda e, b=builder: b(e))
        return None

    def ack(self, eng, qsem, asem):
        self.wait(eng, qsem, self.c[qsem])
        self.c[asem] += 1
        v = self.c[asem]
        self.prog[eng].append(lambda e, s=asem: e.sem_inc(self.sems[s], 1))
        return v

    def barrier(self, dma_flush=()):
        self.nbar += 1
        target = 5 * self.nbar
        flush = {}
        for eng, s in dma_flush:
            flush.setdefault(eng, []).append((s, self.c[s]))
        for eng in ENGINES:
            for s, v in flush.get(eng, []):
                self.wait(eng, s, v)
            self.prog[eng].append(lambda e: e.sem_inc(self.sems["bar"], 1))
            self.wait(eng, "bar", target)


def build_nc(debug=False, preamble="clear_sled"):
    nc = bass.Bass("TRN2", target_bir_lowering=False)

    def din(name, shape, dtype=F32):
        return nc.dram_tensor(name, shape, dtype, kind="ExternalInput").ap()

    resid = din("resid", [LC, D])
    nw_rep = din("nw_rep", [128, D])
    inwT = din("inwT", [D, EC])
    skwT = din("skwT", [D, EC])
    wd1T = din("wd1T", [EC, R])
    wd2T = din("wd2T", [R, EC])
    wbcT = din("wbcT", [EC, 32])
    evec = din("evec", [EC, 24])
    sel_in = din("sel_in", [32, 32 * 128], BF16)
    wguT_sl = din("wguT_sl", [128, 32768])
    wdnT_sl = din("wdnT_sl", [I // NCORE, D])
    owT_sl = din("owT_sl", [E // NCORE, D])
    out = nc.dram_tensor("out", [LC, D], F32, kind="ExternalOutput").ap()

    ag_rn_in = nc.dram_tensor("ag_rn_in", [LC, D], BF16)
    RN_ALL = nc.dram_tensor("RN_ALL", [L, D], BF16, addr_space="Shared")
    ag_wgu_in = nc.dram_tensor("ag_wgu_in", [128, 32768], BF16)
    WGUT = nc.dram_tensor("WGUT", [128, 262144], BF16, addr_space="Shared")
    ag_wdn_in = nc.dram_tensor("ag_wdn_in", [I // NCORE, D], BF16)
    WDNT = nc.dram_tensor("WDNT", [I, D], BF16, addr_space="Shared")
    ag_ow_in = nc.dram_tensor("ag_ow_in", [E // NCORE, D], BF16)
    OWT = nc.dram_tensor("OWT", [E, D], BF16, addr_space="Shared")
    ar_in = nc.dram_tensor("ar_in", [R + 32, L], F32)
    ar_out = nc.dram_tensor("ar_out", [R + 32, L], F32, addr_space="Shared")
    ag_y_in = nc.dram_tensor("ag_y_in", [EC, L], BF16)
    Y_ALL = nc.dram_tensor("Y_ALL", [E, L], BF16, addr_space="Shared")
    XN_SP = nc.dram_tensor("XN_SP", [LC, D], F32)
    XNBF_SP = nc.dram_tensor("XNBF_SP", [LC, D], BF16)
    SK2_SP = nc.dram_tensor("SK2_SP", [EC, L], F32)
    OUT2_SP = nc.dram_tensor("OUT2_SP", [LC, D], F32)
    X_SP = nc.dram_tensor("X_SP", [EC, L], BF16)

    dbg = {}
    if debug:
        for nm, shape, dt in [
            ("dbg_xn", [LC, D], F32), ("dbg_rn", [LC, D], BF16),
            ("dbg_d1bc", [R + 32, L], F32), ("dbg_x", [EC, L], BF16),
            ("dbg_sk2", [EC, L], F32), ("dbg_delta", [EC, L], F32),
            ("dbg_y", [EC, L], BF16), ("dbg_out2", [LC, D], F32),
            ("dbg_scan", [128, 6 * CH], F32),
        ]:
            dbg[nm] = nc.dram_tensor(nm, shape, dt, kind="ExternalOutput").ap()

    semnames = ["bar", "dsy", "dgp", "asy", "agp", "mm", "vv", "aa", "gg",
                "cc_rn", "cc_wgu", "cc_wdn", "cc_ow", "cc_ar", "cc_y"]

    with contextlib.ExitStack() as g:
        p = P(nc, g, semnames)
        W, EM = p.wait, p.emit

        def gack():
            return p.ack("gpsimd", "dgp", "agp")

        def sack():
            return p.ack("sync", "dsy", "asy")

        nums = sorted(s.num for s in p.sems.values())
        assert nums == list(range(nums[0], nums[0] + len(nums))), nums
        if preamble == "clear_sled":
            for _ in range(40):
                nc.gpsimd.nop(cycle_cnt=2400)
            nc.gpsimd.sem_clear(range(nums[0], nums[-1] + 1))
            for eng in (nc.sync, nc.vector, nc.scalar, nc.tensor):
                for _ in range(120):
                    eng.nop(cycle_cnt=2400)
            nc.all_engine_barrier()

        sbg = lambda name, shape, dtype=F32: g.enter_context(nc.sbuf_tensor(name, shape, dtype))
        evs = sbg("evs", [128, NET * 24])
        asb = sbg("asb", [128, NET * 16])
        carry = sbg("carry", [128, NET * N])

        def ecol(et, c):
            return evs[:, et * 24 + c:et * 24 + c + 1]

        with contextlib.ExitStack() as ph:
            sb = lambda name, shape, dtype=F32: ph.enter_context(nc.sbuf_tensor(name, shape, dtype))
            nwr = sb("nwr", [128, D])
            rsd = [sb(f"rsd{i}", [128, D]) for i in range(2)]
            sq_scr = sb("sq_scr", [128, D])
            xn_t = [sb(f"xn_t{i}", [128, D]) for i in range(2)]
            rn_bf = [sb(f"rn_bf{i}", [128, D], BF16) for i in range(2)]
            xn_bf = [sb(f"xn_bf{i}", [128, D], BF16) for i in range(2)]
            stat = sb("stat", [128, 8])
            wf32 = sb("wf32", [128, 4096])
            wbf = sb("wbf", [128, 4096], BF16)

            EM("gpsimd", lambda e: e.dma_start(out=nwr[:, :], in_=nw_rep), inc="dgp", inck=16)
            EM("gpsimd", lambda e: e.dma_start(
                out=evs[:, :].rearrange("p (t c) -> p t c", t=NET),
                in_=evec.rearrange("(t p) c -> p t c", p=128)), inc="dgp", inck=16)
            a_ev = gack()
            W("scalar", "agp", a_ev)
            EM("scalar", lambda e: e.activation(
                asb[:, :].rearrange("p (t n) -> p t n", t=NET),
                evs[:, :].rearrange("p (t c) -> p t c", t=NET)[:, :, 8:24], AF.Exp), inc="aa")
            W("vector", "aa", p.c["aa"])
            EM("vector", lambda e: e.tensor_scalar_mul(asb[:, :], asb[:, :], -1.0), inc="vv")

            rsd_free = {}
            spill_ack = {}
            for t in range(8):
                b = t % 2
                if t >= 2:
                    W("gpsimd", "vv", rsd_free[t - 2])
                EM("gpsimd", lambda e, b=b, t=t: e.dma_start(
                    out=rsd[b][:, :], in_=resid[t * 128:(t + 1) * 128, :]), inc="dgp", inck=16)
                a_ld = gack()
                W("vector", "agp", a_ld)
                rs1 = EM("vector", lambda e, b=b: e.reduce_sum(
                    stat[:, 0:1], rsd[b][:, :], axis=mybir.AxisListType.X), inc="vv")
                W("vector", "vv", rs1)
                EM("vector", lambda e: e.tensor_scalar_mul(stat[:, 0:1], stat[:, 0:1], 1.0 / D))
                sqv = EM("vector", lambda e, b=b: e.tensor_mul(sq_scr[:, :], rsd[b][:, :], rsd[b][:, :]),
                         inc="vv")
                W("vector", "vv", sqv)
                rs2 = EM("vector", lambda e: e.reduce_sum(
                    stat[:, 1:2], sq_scr[:, :], axis=mybir.AxisListType.X), inc="vv")
                W("vector", "vv", rs2)
                ts2 = EM("vector", lambda e: e.tensor_scalar(
                    stat[:, 2:3], stat[:, 1:2], 1.0 / D, EPS, OP.mult, OP.add), inc="vv")
                EM("vector", lambda e: e.tensor_mul(stat[:, 7:8], stat[:, 0:1], stat[:, 0:1]))
                W("vector", "vv", ts2)
                vv1 = EM("vector", lambda e: e.tensor_sub(stat[:, 2:3], stat[:, 2:3], stat[:, 7:8]), inc="vv")
                W("scalar", "vv", vv1)
                EM("scalar", lambda e: e.activation(stat[:, 3:4], stat[:, 2:3], AF.Ln))
                aa1 = EM("scalar", lambda e: e.activation(stat[:, 3:4], stat[:, 3:4], AF.Exp, scale=-0.5), inc="aa")
                W("vector", "aa", aa1)
                if t >= 2:
                    W("vector", "agp", spill_ack[t - 2])
                xnv = EM("vector", lambda e, b=b: e.tensor_scalar(
                    xn_t[b][:, :], rsd[b][:, :], stat[:, 0:1], stat[:, 3:4], OP.subtract, OP.mult), inc="vv")
                rsd_free[t] = xnv
                W("vector", "vv", xnv)
                sqv2 = EM("vector", lambda e, b=b: e.tensor_mul(sq_scr[:, :], xn_t[b][:, :], xn_t[b][:, :]),
                          inc="vv")
                W("vector", "vv", sqv2)
                rs3 = EM("vector", lambda e: e.reduce_sum(
                    stat[:, 4:5], sq_scr[:, :], axis=mybir.AxisListType.X), inc="vv")
                W("vector", "vv", rs3)
                vv2 = EM("vector", lambda e: e.tensor_scalar(
                    stat[:, 5:6], stat[:, 4:5], 1.0 / D, EPS, OP.mult, OP.add), inc="vv")
                W("scalar", "vv", vv2)
                EM("scalar", lambda e: e.activation(stat[:, 6:7], stat[:, 5:6], AF.Ln))
                aa2 = EM("scalar", lambda e: e.activation(stat[:, 6:7], stat[:, 6:7], AF.Exp, scale=-0.5), inc="aa")
                W("vector", "aa", aa2)
                rnv = EM("vector", lambda e, b=b: e.scalar_tensor_tensor(
                    rn_bf[b][:, :], xn_t[b][:, :], stat[:, 6:7], nwr[:, :], OP.mult, OP.mult), inc="vv")
                W("vector", "vv", rnv)
                vd = EM("vector", lambda e, b=b: e.tensor_copy(xn_bf[b][:, :], xn_t[b][:, :]), inc="vv")
                W("gpsimd", "vv", vd)
                EM("gpsimd", lambda e, b=b, t=t: e.dma_start(
                    out=XN_SP.ap()[t * 128:(t + 1) * 128, :], in_=xn_t[b][:, :]), inc="dgp", inck=16)
                EM("gpsimd", lambda e, b=b, t=t: e.dma_start(
                    out=ag_rn_in.ap()[t * 128:(t + 1) * 128, :], in_=rn_bf[b][:, :]), inc="dgp", inck=16)
                EM("gpsimd", lambda e, b=b, t=t: e.dma_start(
                    out=XNBF_SP.ap()[t * 128:(t + 1) * 128, :], in_=xn_bf[b][:, :]), inc="dgp", inck=16)
                spill_ack[t] = gack()

            cast_state = {"k": 0, "store": {}}

            def cast_chunks(src_ap, dst_dram, nrows, ncols, colchunk):
                for rt in range(nrows // 128):
                    for c0 in range(0, ncols, colchunk):
                        k = cast_state["k"]
                        cast_state["k"] += 1
                        kb = k % 2
                        q, qs, ak = ("gpsimd", "dgp", "agp") if kb == 0 else ("sync", "dsy", "asy")
                        EM(q, lambda e, rt=rt, c0=c0, cc=colchunk, kb=kb: e.dma_start(
                            out=wf32[:, kb * 2048:kb * 2048 + cc // 2],
                            in_=src_ap[rt * 128:(rt + 1) * 128, c0:c0 + cc // 2]), inc=qs, inck=16)
                        a_ld = p.ack(q, qs, ak)
                        W("vector", ak, a_ld)
                        if (k - 2) in cast_state["store"]:
                            sq2, sv2 = cast_state["store"][k - 2]
                            W("vector", sq2, sv2)
                        cv = EM("vector", lambda e, cc=colchunk, kb=kb: e.tensor_copy(
                            wbf[:, kb * 2048:kb * 2048 + cc // 2], wf32[:, kb * 2048:kb * 2048 + cc // 2]),
                            inc="vv")
                        W(q, "vv", cv)
                        EM(q, lambda e, rt=rt, c0=c0, cc=colchunk, kb=kb: e.dma_start(
                            out=dst_dram.ap()[rt * 128:(rt + 1) * 128, c0:c0 + cc // 2],
                            in_=wbf[:, kb * 2048:kb * 2048 + cc // 2]), inc=qs, inck=16)
                        cast_state["store"][k] = (ak, p.ack(q, qs, ak))
                        k = cast_state["k"]
                        cast_state["k"] += 1
                        kb = k % 2
                        q, qs, ak = ("gpsimd", "dgp", "agp") if kb == 0 else ("sync", "dsy", "asy")
                        EM(q, lambda e, rt=rt, c0=c0, cc=colchunk, kb=kb: e.dma_start(
                            out=wf32[:, kb * 2048:kb * 2048 + cc // 2],
                            in_=src_ap[rt * 128:(rt + 1) * 128, c0 + cc // 2:c0 + cc]), inc=qs, inck=16)
                        a_ld = p.ack(q, qs, ak)
                        W("vector", ak, a_ld)
                        if (k - 2) in cast_state["store"]:
                            sq2, sv2 = cast_state["store"][k - 2]
                            W("vector", sq2, sv2)
                        cv = EM("vector", lambda e, cc=colchunk, kb=kb: e.tensor_copy(
                            wbf[:, kb * 2048:kb * 2048 + cc // 2], wf32[:, kb * 2048:kb * 2048 + cc // 2]),
                            inc="vv")
                        W(q, "vv", cv)
                        EM(q, lambda e, rt=rt, c0=c0, cc=colchunk, kb=kb: e.dma_start(
                            out=dst_dram.ap()[rt * 128:(rt + 1) * 128, c0 + cc // 2:c0 + cc],
                            in_=wbf[:, kb * 2048:kb * 2048 + cc // 2]), inc=qs, inck=16)
                        cast_state["store"][k] = (ak, p.ack(q, qs, ak))

            cast_chunks(wguT_sl, ag_wgu_in, 128, 32768, 4096)
            cast_chunks(wdnT_sl, ag_wdn_in, I // NCORE, D, 2048)
            cast_chunks(owT_sl, ag_ow_in, E // NCORE, D, 2048)

            W("gpsimd", "dgp", p.c["dgp"])
            W("gpsimd", "asy", p.c["asy"])
            for iap, oap, sem in [
                (ag_rn_in, RN_ALL, "cc_rn"), (ag_wgu_in, WGUT, "cc_wgu"),
                (ag_wdn_in, WDNT, "cc_wdn"), (ag_ow_in, OWT, "cc_ow"),
            ]:
                EM("gpsimd", lambda e, iap=iap, oap=oap: e.collective_compute(
                    "AllGather", OP.bypass, replica_groups=[list(range(NCORE))],
                    ins=[iap.ap().opt()], outs=[oap.ap().opt()]), inc=sem)
            p.barrier(dma_flush=[("gpsimd", "dgp"), ("sync", "dsy")])

        with contextlib.ExitStack() as ph:
            sb = lambda name, shape, dtype=F32: ph.enter_context(nc.sbuf_tensor(name, shape, dtype))
            psb = lambda name, shape: ph.enter_context(nc.psum_tensor(name, shape, F32))
            wstage = sb("wstage", [128, 16 * 512])
            inw_bf = sb("inw_bf", [128, 16 * 512], BF16)
            skw_bf = sb("skw_bf", [128, 16 * 512], BF16)
            wd1_bf = sb("wd1_bf", [128, NET * 128], BF16)
            wbc_bf = sb("wbc_bf", [128, NET * 32], BF16)
            rhs = [sb(f"rhs{i}", [128, 16 * 512], BF16) for i in range(2)]
            xin_buf = [sb(f"xinb{et}", [128, 515]) for et in range(NET)]
            acc = [sb(f"acc{et}", [128, 512]) for et in range(NET)]
            sk2_t = [[sb(f"sk2t{i}_{et}", [128, 512]) for et in range(NET)] for i in range(2)]
            d1e = [sb(f"d1e{i}", [128, 512]) for i in range(2)]
            bce = [sb(f"bce{i}", [128, 512]) for i in range(2)]
            x_bf = sb("x_bf", [128, NET * L], BF16)
            ps_xin = [psb(f"ps_xin{i}", [128, 512]) for i in range(2)]
            ps_skip = [psb(f"ps_skip{i}", [128, 512]) for i in range(2)]
            ps_d1 = [psb(f"ps_d1{i}", [128, 512]) for i in range(2)]

            def load_cast(src, w, sl, nfree):
                EM("gpsimd", lambda e, src=src, sl=sl, nfree=nfree: e.dma_start(
                    out=wstage[:, 0:nfree].rearrange("p (t c) -> p t c", t=sl),
                    in_=src.rearrange("(t p) c -> p t c", p=128)), inc="dgp", inck=16)
                a_ld = gack()
                W("vector", "agp", a_ld)
                cv = EM("vector", lambda e, w=w, nfree=nfree: e.tensor_copy(w[:, 0:nfree], wstage[:, 0:nfree]),
                        inc="vv")
                W("gpsimd", "vv", cv)

            load_cast(inwT, inw_bf, 16, 16 * 512)
            load_cast(skwT, skw_bf, 16, 16 * 512)
            load_cast(wd1T, wd1_bf, NET, NET * 128)
            load_cast(wbcT, wbc_bf, NET, NET * 32)
            for et in range(NET - 1):
                EM("vector", lambda e, et=et: e.memset(xin_buf[et][:, 0:3], 0.0))
            v_w = EM("vector", lambda e: e.memset(xin_buf[NET - 1][:, 0:3], 0.0), inc="vv")

            W("sync", "cc_rn", 1)
            NLC = 16
            mm_xin, mm_skip, evac_xin, evac_skip = {}, {}, {}, {}
            xsilu, sk2_ack = {}, {}

            def rhs_load(lc):
                for dt in range(16):
                    EM("sync", lambda e, lc=lc, dt=dt: e.dma_start_transpose(
                        rhs[lc % 2][:, dt * 512:(dt + 1) * 512],
                        RN_ALL.ap()[lc * 512:(lc + 1) * 512, dt * 128:(dt + 1) * 128]),
                        inc="dsy", inck=16)

            rhs_load(0)
            for lc in range(NLC):
                b = lc % 2
                if lc + 1 < NLC:
                    if lc >= 1:
                        W("sync", "mm", mm_skip[(lc - 1, NET - 1)])
                    rhs_load(lc + 1)
                a_rhs = sack()
                W("tensor", "asy", a_rhs)
                if lc == 0:
                    W("tensor", "vv", v_w)
                for et in range(NET):
                    pb = et % 2
                    prev = (lc, et - 2) if et >= 2 else (lc - 1, et + 2)
                    if prev in evac_xin:
                        W("tensor", "vv", evac_xin[prev])
                    for dt in range(16):
                        mm_xin[(lc, et)] = EM("tensor", lambda e, et=et, dt=dt, b=b, pb=pb: e.matmul(
                            ps_xin[pb][:, :], inw_bf[:, dt * 512 + et * 128:dt * 512 + (et + 1) * 128],
                            rhs[b][:, dt * 512:(dt + 1) * 512], start=(dt == 0), stop=(dt == 15)),
                            inc="mm" if dt == 15 else None)
                    if prev in evac_skip:
                        W("tensor", "aa", evac_skip[prev])
                    for dt in range(16):
                        mm_skip[(lc, et)] = EM("tensor", lambda e, et=et, dt=dt, b=b, pb=pb: e.matmul(
                            ps_skip[pb][:, :], skw_bf[:, dt * 512 + et * 128:dt * 512 + (et + 1) * 128],
                            rhs[b][:, dt * 512:(dt + 1) * 512], start=(dt == 0), stop=(dt == 15)),
                            inc="mm" if dt == 15 else None)

                    W("vector", "mm", mm_xin[(lc, et)])
                    if (lc - 1, et) in xsilu:
                        W("vector", "aa", xsilu[(lc - 1, et)])
                    evv = EM("vector", lambda e, et=et, pb=pb: e.tensor_copy(
                        xin_buf[et][:, 3:515], ps_xin[pb][:, :]), inc="vv")
                    W("vector", "vv", evv)
                    cv0 = EM("vector", lambda e, et=et: e.tensor_scalar(
                        acc[et][:, :], xin_buf[et][:, 0:512], ecol(et, 0), None, OP.mult), inc="vv")
                    for k in range(1, 4):
                        W("vector", "vv", cv0)
                        cv0 = EM("vector", lambda e, et=et, k=k: e.scalar_tensor_tensor(
                            acc[et][:, :], xin_buf[et][:, k:k + 512], ecol(et, k),
                            acc[et][:, :], OP.mult, OP.add), inc="vv")
                    evac_xin[(lc, et)] = cv0
                    EM("vector", lambda e, et=et: e.tensor_copy(
                        xin_buf[et][:, 0:3], xin_buf[et][:, 512:515]), inc="vv")
                    W("scalar", "vv", evac_xin[(lc, et)])
                    xsilu[(lc, et)] = EM("scalar", lambda e, et=et, lc=lc: e.activation(
                        x_bf[:, et * L + lc * 512:et * L + (lc + 1) * 512], acc[et][:, :], AF.Silu,
                        bias=ecol(et, 4)), inc="aa")
                    W("scalar", "mm", mm_skip[(lc, et)])
                    if (lc - 2, et) in sk2_ack:
                        W("scalar", "agp", sk2_ack[(lc - 2, et)])
                    evac_skip[(lc, et)] = EM("scalar", lambda e, et=et, pb=pb, b=b: e.activation(
                        sk2_t[b][et][:, :], ps_skip[pb][:, :], AF.Silu), inc="aa")
                    W("gpsimd", "aa", evac_skip[(lc, et)])
                    EM("gpsimd", lambda e, et=et, lc=lc, b=b: e.dma_start(
                        out=SK2_SP.ap()[et * 128:(et + 1) * 128, lc * 512:(lc + 1) * 512],
                        in_=sk2_t[b][et][:, :]), inc="dgp", inck=16)
                    sk2_ack[(lc, et)] = gack()

            x_ready = p.c["aa"]

            d1_done, d1_ack = {}, {}
            W("tensor", "aa", x_ready)
            for lc in range(NLC):
                b = lc % 2
                if lc >= 2:
                    W("tensor", "vv", d1_done[lc - 2])
                m1 = m2 = None
                for et in range(NET):
                    m1 = EM("tensor", lambda e, et=et, lc=lc, b=b: e.matmul(
                        ps_d1[b][:, :], wd1_bf[:, et * 128:(et + 1) * 128],
                        x_bf[:, et * L + lc * 512:et * L + (lc + 1) * 512],
                        start=(et == 0), stop=(et == NET - 1)),
                        inc="mm" if et == NET - 1 else None)
                for et in range(NET):
                    m2 = EM("tensor", lambda e, et=et, lc=lc, b=b: e.matmul(
                        ps_skip[b][0:32, :], wbc_bf[:, et * 32:(et + 1) * 32],
                        x_bf[:, et * L + lc * 512:et * L + (lc + 1) * 512],
                        start=(et == 0), stop=(et == NET - 1)),
                        inc="mm" if et == NET - 1 else None)
                W("vector", "mm", m1)
                if lc >= 2:
                    W("vector", "agp", d1_ack[lc - 2])
                EM("vector", lambda e, b=b: e.tensor_copy(d1e[b][:, :], ps_d1[b][:, :]))
                W("vector", "mm", m2)
                d1_done[lc] = EM("vector", lambda e, b=b: e.tensor_copy(bce[b][0:32, :], ps_skip[b][0:32, :]),
                                 inc="vv")
                W("gpsimd", "vv", d1_done[lc])
                EM("gpsimd", lambda e, lc=lc, b=b: e.dma_start(
                    out=ar_in.ap()[0:R, lc * 512:(lc + 1) * 512], in_=d1e[b][:, :]), inc="dgp", inck=16)
                EM("gpsimd", lambda e, lc=lc, b=b: e.dma_start(
                    out=ar_in.ap()[R:R + 32, lc * 512:(lc + 1) * 512], in_=bce[b][0:32, :]), inc="dgp", inck=16)
                d1_ack[lc] = gack()
            W("gpsimd", "dgp", p.c["dgp"])
            EM("gpsimd", lambda e: e.collective_compute(
                "AllReduce", OP.add, replica_groups=[list(range(NCORE))],
                ins=[ar_in.ap().opt()], outs=[ar_out.ap().opt()]), inc="cc_ar")
            W("gpsimd", "aa", x_ready)
            for et in range(NET):
                EM("gpsimd", lambda e, et=et: e.dma_start(
                    out=X_SP.ap()[et * 128:(et + 1) * 128, :], in_=x_bf[:, et * L:(et + 1) * L]),
                    inc="dgp", inck=16)
            p.barrier(dma_flush=[("gpsimd", "dgp"), ("sync", "dsy")])

        with contextlib.ExitStack() as ph:
            sb = lambda name, shape, dtype=F32: ph.enter_context(nc.sbuf_tensor(name, shape, dtype))
            psb = lambda name, shape: ph.enter_context(nc.psum_tensor(name, shape, F32))
            d1t_f = sb("d1t_f", [128, CH])
            d1t_bf = sb("d1t_bf", [128, CH], BF16)
            bmc_bf = sb("bmc_bf", [32, L], BF16)
            wd2_bf = sb("wd2_bf", [128, EC], BF16)
            delta = sb("delta", [128, NET * CH])
            xc_bf = sb("xc_bf", [128, NET * CH], BF16)
            u = sb("u", [128, NET * CH])
            y_acc = sb("y_acc", [128, NET * CH])
            sk2c = [sb(f"sk2c{i}", [128, CH]) for i in range(2)]
            p_buf = sb("p_buf", [128, 2 * CH])
            ug_buf = sb("ug_buf", [128, 2 * CH])
            h_buf = sb("h_buf", [128, 2 * CH])
            prod = sb("prod", [128, CH])
            brep = sb("brep", [128, 2 * CH])
            crep = sb("crep", [128, 2 * CH])
            t0 = sb("t0", [128, CH])
            t1 = sb("t1", [128, CH])
            y_bf = [sb(f"y_bf{i}", [128, CH], BF16) for i in range(2)]
            sel_bf = sb("sel_bf", [32, 32 * 128], BF16)
            ps_dp = psb("ps_dp", [128, CH])
            ps_b = psb("ps_b", [128, CH])
            ps_c = psb("ps_c", [128, CH])

            EM("gpsimd", lambda e: e.dma_start(out=sel_bf[:, :], in_=sel_in), inc="dgp", inck=16)
            EM("gpsimd", lambda e: e.dma_start(out=d1t_f[:, 0:EC], in_=wd2T), inc="dgp", inck=16)
            a_w = gack()
            W("vector", "agp", a_w)
            wcv = EM("vector", lambda e: e.tensor_copy(wd2_bf[:, :], d1t_f[:, 0:EC]), inc="vv")
            W("sync", "cc_ar", 1)
            W("sync", "vv", wcv)
            for cb in range(NCH):
                EM("sync", lambda e, cb=cb: e.dma_start(
                    out=t0[0:32, 0:CH], in_=ar_out.ap()[R:R + 32, cb * CH:(cb + 1) * CH]), inc="dsy", inck=16)
                a_b = sack()
                W("vector", "asy", a_b)
                bv = EM("vector", lambda e, cb=cb: e.tensor_copy(
                    bmc_bf[:, cb * CH:(cb + 1) * CH], t0[0:32, 0:CH]), inc="vv")
                W("sync", "vv", bv)
            W("tensor", "agp", a_w)

            scan_done, brep_ok, pg_done = {}, {}, {}
            y_ack, fixv = {}, {}
            sk_ack = {}
            delta_ok = {}
            for cnk in range(NCH):
                c0 = cnk * CH
                if cnk >= 1:
                    W("sync", "vv", delta_ok[(cnk - 1, 0)])
                EM("sync", lambda e, c0=c0: e.dma_start(
                    out=d1t_f[:, :], in_=ar_out.ap()[0:R, c0:c0 + CH]), inc="dsy", inck=16)
                a_d1 = sack()
                W("vector", "asy", a_d1)
                dcv = EM("vector", lambda e: e.tensor_copy(d1t_bf[:, :], d1t_f[:, :]), inc="vv")
                W("tensor", "vv", dcv)
                for et in range(NET):
                    mdp = None
                    for hh in range(2):
                        mdp = EM("tensor", lambda e, et=et, hh=hh: e.matmul(
                            ps_dp[:, hh * 512:(hh + 1) * 512], wd2_bf[:, et * 128:(et + 1) * 128],
                            d1t_bf[:, hh * 512:(hh + 1) * 512], start=True, stop=True),
                            inc="mm" if hh == 1 else None)
                    W("scalar", "mm", mdp)
                    if (cnk, et - 1) in delta_ok:
                        W("scalar", "vv", delta_ok[(cnk, et - 1)])
                    elif cnk >= 1:
                        W("scalar", "vv", fixv[(cnk - 1, NET - 1)])
                    EM("scalar", lambda e, et=et: e.activation(t0[:, :], ps_dp[:, :], AF.Relu, bias=ecol(et, 5)))
                    EM("scalar", lambda e, et=et: e.activation(t1[:, :], ps_dp[:, :], AF.Abs, bias=ecol(et, 5)))
                    EM("scalar", lambda e: e.activation(t1[:, :], t1[:, :], AF.Exp, scale=-1.0))
                    sdn = EM("scalar", lambda e: e.activation(t1[:, :], t1[:, :], AF.Ln, bias=1.0), inc="aa")
                    W("tensor", "aa", sdn)
                    W("vector", "aa", sdn)
                    delta_ok[(cnk, et)] = EM("vector", lambda e, et=et: e.tensor_add(
                        delta[:, et * CH:(et + 1) * CH], t0[:, :], t1[:, :]), inc="vv")
                if cnk >= 1:
                    W("gpsimd", "vv", fixv[(cnk - 1, NET - 1)])
                EM("gpsimd", lambda e, c0=c0: e.dma_start(
                    out=xc_bf[:, :].rearrange("p (et l) -> p et l", et=NET),
                    in_=X_SP.ap()[:, c0:c0 + CH].rearrange("(et p) l -> p et l", p=128)),
                    inc="dgp", inck=16)
                a_x = gack()
                if cnk >= 1:
                    W("vector", "gg", pg_done[(cnk - 1, N - 1, NET - 1)])
                W("vector", "agp", a_x)
                for et in range(NET):
                    EM("vector", lambda e, et=et: e.tensor_copy(t1[:, :], xc_bf[:, et * CH:(et + 1) * CH]))
                    EM("vector", lambda e, et=et: e.tensor_mul(
                        u[:, et * CH:(et + 1) * CH], delta[:, et * CH:(et + 1) * CH], t1[:, :]))
                uv = EM("vector", lambda e: e.memset(y_acc[:, :], 0.0), inc="vv")

                for n in range(N):
                    nb = n % 2
                    if (cnk, n - 1) in brep_ok:
                        W("tensor", "aa", brep_ok[(cnk, n - 1)])
                    elif cnk >= 1 and n == 0:
                        W("tensor", "aa", brep_ok[(cnk - 1, N - 1)])
                    for hh in range(2):
                        EM("tensor", lambda e, n=n, hh=hh, c0=c0: e.matmul(
                            ps_b[:, hh * 512:(hh + 1) * 512], sel_bf[0:32, n * 128:(n + 1) * 128],
                            bmc_bf[0:32, c0 + hh * 512:c0 + (hh + 1) * 512], start=True, stop=True))
                    mbc = None
                    for hh in range(2):
                        mbc = EM("tensor", lambda e, n=n, hh=hh, c0=c0: e.matmul(
                            ps_c[:, hh * 512:(hh + 1) * 512], sel_bf[0:32, (16 + n) * 128:(17 + n) * 128],
                            bmc_bf[0:32, c0 + hh * 512:c0 + (hh + 1) * 512], start=True, stop=True),
                            inc="mm" if hh == 1 else None)
                    W("scalar", "mm", mbc)
                    key = (cnk, n - 2, NET - 1) if n >= 2 else ((cnk - 1, N - 2 + n, NET - 1) if cnk >= 1 else None)
                    if key in pg_done:
                        W("scalar", "gg", pg_done[key])
                    skey = (cnk, (n - 2) * NET + NET - 1) if n >= 2 else (
                        (cnk - 1, (N - 2 + n) * NET + NET - 1) if cnk >= 1 else None)
                    if skey in scan_done:
                        W("scalar", "vv", scan_done[skey])
                    EM("scalar", lambda e, nb=nb: e.activation(brep[:, nb * CH:(nb + 1) * CH], ps_b[:, :],
                                                               AF.Identity))
                    bcv = EM("scalar", lambda e, nb=nb: e.activation(crep[:, nb * CH:(nb + 1) * CH], ps_c[:, :],
                                                                     AF.Identity), inc="aa")
                    brep_ok[(cnk, n)] = bcv

                    for et in range(NET):
                        s = n * NET + et
                        sb_ = s % 2
                        pkey = (cnk, s - 2) if s >= 2 else ((cnk - 1, N * NET - 2 + s) if cnk >= 1 else None)
                        W("scalar", "vv", uv)
                        if pkey in scan_done:
                            W("scalar", "vv", scan_done[pkey])
                        pa = EM("scalar", lambda e, et=et, n=n, sb_=sb_: e.activation(
                            p_buf[:, sb_ * CH:(sb_ + 1) * CH], delta[:, et * CH:(et + 1) * CH], AF.Exp,
                            scale=asb[:, et * 16 + n:et * 16 + n + 1]), inc="aa")
                        W("gpsimd", "aa", bcv)
                        W("gpsimd", "vv", uv)
                        if pkey in scan_done:
                            W("gpsimd", "vv", scan_done[pkey])
                        pg = EM("gpsimd", lambda e, et=et, nb=nb, sb_=sb_: e.tensor_mul(
                            ug_buf[:, sb_ * CH:(sb_ + 1) * CH], u[:, et * CH:(et + 1) * CH],
                            brep[:, nb * CH:(nb + 1) * CH]), inc="gg")
                        pg_done[(cnk, n, et)] = pg
                        W("vector", "aa", pa)
                        W("vector", "gg", pg)
                        cidx = et * N + n
                        if cnk == 0:
                            sv0 = EM("vector", lambda e, sb_=sb_: e.tensor_tensor_scan(
                                h_buf[:, sb_ * CH:(sb_ + 1) * CH], p_buf[:, sb_ * CH:(sb_ + 1) * CH],
                                ug_buf[:, sb_ * CH:(sb_ + 1) * CH], 0.0, OP.mult, OP.add), inc="vv")
                        else:
                            sv0 = EM("vector", lambda e, sb_=sb_, cidx=cidx: e.tensor_tensor_scan(
                                h_buf[:, sb_ * CH:(sb_ + 1) * CH], p_buf[:, sb_ * CH:(sb_ + 1) * CH],
                                ug_buf[:, sb_ * CH:(sb_ + 1) * CH], carry[:, cidx:cidx + 1], OP.mult, OP.add),
                                inc="vv")
                        W("vector", "vv", sv0)
                        if debug and cnk == 0 and n == 0 and et == 0:
                            W("gpsimd", "vv", sv0)
                            for j, src_ap in enumerate([
                                    lambda: p_buf[:, 0:CH], lambda: ug_buf[:, 0:CH],
                                    lambda: h_buf[:, 0:CH], lambda: u[:, 0:CH],
                                    lambda: brep[:, 0:CH]]):
                                EM("gpsimd", lambda e, j=j, src_ap=src_ap: e.dma_start(
                                    out=dbg["dbg_scan"][:, j * CH:(j + 1) * CH], in_=src_ap()),
                                    inc="dgp", inck=16)
                            EM("gpsimd", lambda e: e.dma_start(
                                out=dbg["dbg_scan"][:, 5 * CH:6 * CH], in_=crep[:, 0:CH]),
                                inc="dgp", inck=16)
                            scn_a = gack()
                            W("scalar", "agp", scn_a)
                            W("vector", "agp", scn_a)
                            W("gpsimd", "agp", scn_a)
                        EM("vector", lambda e, sb_=sb_, cidx=cidx: e.tensor_copy(
                            carry[:, cidx:cidx + 1], h_buf[:, sb_ * CH + CH - 1:sb_ * CH + CH]))
                        EM("vector", lambda e, sb_=sb_, nb=nb: e.tensor_mul(
                            prod[:, :], h_buf[:, sb_ * CH:(sb_ + 1) * CH], crep[:, nb * CH:(nb + 1) * CH]))
                        scan_done[(cnk, s)] = EM("vector", lambda e, et=et: e.tensor_add(
                            y_acc[:, et * CH:(et + 1) * CH], y_acc[:, et * CH:(et + 1) * CH], prod[:, :]),
                            inc="vv")

                for et in range(NET):
                    yb = et % 2
                    skkey = (cnk, et - 2) if et >= 2 else ((cnk - 1, et + 2) if cnk >= 1 else None)
                    if skkey in fixv:
                        W("gpsimd", "vv", fixv[skkey])
                    EM("gpsimd", lambda e, et=et, c0=c0, yb=yb: e.dma_start(
                        out=sk2c[yb][:, :], in_=SK2_SP.ap()[et * 128:(et + 1) * 128, c0:c0 + CH]),
                        inc="dgp", inck=16)
                    sk_ack[(cnk, et)] = gack()
                    t1v = EM("vector", lambda e, et=et: e.tensor_copy(
                        t1[:, :], xc_bf[:, et * CH:(et + 1) * CH]), inc="vv")
                    W("vector", "vv", t1v)
                    fx0 = EM("vector", lambda e, et=et: e.scalar_tensor_tensor(
                        y_acc[:, et * CH:(et + 1) * CH], t1[:, :], ecol(et, 6),
                        y_acc[:, et * CH:(et + 1) * CH], OP.mult, OP.add), inc="vv")
                    W("vector", "vv", fx0)
                    W("vector", "agp", sk_ack[(cnk, et)])
                    ykey = (cnk, et - 2) if et >= 2 else ((cnk - 1, et + 2) if cnk >= 1 else None)
                    if ykey in y_ack:
                        W("vector", "agp", y_ack[ykey])
                    fixv[(cnk, et)] = EM("vector", lambda e, et=et, yb=yb: e.tensor_mul(
                        y_bf[yb][:, :], y_acc[:, et * CH:(et + 1) * CH], sk2c[yb][:, :]), inc="vv")
                    W("gpsimd", "vv", fixv[(cnk, et)])
                    EM("gpsimd", lambda e, et=et, c0=c0, yb=yb: e.dma_start(
                        out=ag_y_in.ap()[et * 128:(et + 1) * 128, c0:c0 + CH], in_=y_bf[yb][:, :]),
                        inc="dgp", inck=16)
                    y_ack[(cnk, et)] = gack()
                if debug:
                    W("gpsimd", "vv", p.c["vv"])
                    for et in range(NET):
                        EM("gpsimd", lambda e, et=et, c0=c0: e.dma_start(
                            out=dbg["dbg_delta"][et * 128:(et + 1) * 128, c0:c0 + CH],
                            in_=delta[:, et * CH:(et + 1) * CH]), inc="dgp", inck=16)
                    dbg_a = gack()
                    W("vector", "agp", dbg_a)

            W("gpsimd", "dgp", p.c["dgp"])
            EM("gpsimd", lambda e: e.collective_compute(
                "AllGather", OP.bypass, replica_groups=[list(range(NCORE))],
                ins=[ag_y_in.ap().opt()], outs=[Y_ALL.ap().opt()]), inc="cc_y")
            p.barrier(dma_flush=[("gpsimd", "dgp"), ("sync", "dsy")])

        with contextlib.ExitStack() as ph:
            sb = lambda name, shape, dtype=F32: ph.enter_context(nc.sbuf_tensor(name, shape, dtype))
            psb = lambda name, shape: ph.enter_context(nc.psum_tensor(name, shape, F32))
            xnt = sb("xnt", [128, 16 * 512], BF16)
            wg_g = [sb(f"wg_g{i}", [128, 16 * 128], BF16) for i in range(2)]
            wg_u = [sb(f"wg_u{i}", [128, 16 * 128], BF16) for i in range(2)]
            sg = [sb(f"sg{i}", [128, 512]) for i in range(2)]
            h_bf = sb("h_bf", [128, 64 * 512], BF16)
            wdn_t = [sb(f"wdn_t{i}", [128, 512], BF16) for i in range(2)]
            o2_t = sb("o2_t", [128, 512])
            ps_g = [psb(f"ps_g{i}", [128, 512]) for i in range(2)]
            ps_u = [psb(f"ps_u{i}", [128, 512]) for i in range(2)]
            ps_dn = [psb(f"ps_dn{i}", [128, 512]) for i in range(4)]

            W("sync", "cc_wgu", 1)
            W("sync", "cc_wdn", 1)
            gu_mm, h_done, dn_mm, dn_evac = {}, {}, {}, {}
            o2_ack = {}
            for lh in range(2):
                l0 = lh * 512
                if lh == 1:
                    W("sync", "mm", gu_mm[(0, 63)])
                for dt in range(16):
                    EM("sync", lambda e, dt=dt, l0=l0: e.dma_start_transpose(
                        xnt[:, dt * 512:(dt + 1) * 512],
                        XNBF_SP.ap()[l0:l0 + 512, dt * 128:(dt + 1) * 128]), inc="dsy", inck=16)
                a_xnt = sack()
                for it in range(64):
                    b = it % 2
                    q = "sync" if b == 0 else "gpsimd"
                    qs, ak = ("dsy", "asy") if b == 0 else ("dgp", "agp")
                    if it >= 2:
                        W(q, "mm", gu_mm[(lh, it - 2)])
                    EM(q, lambda e, it=it, b=b: e.dma_start(
                        out=wg_g[b][:, :],
                        in_=WGUT.ap()[it:it + 1, :].rearrange("r (p c) -> (r p) c", p=128)),
                        inc=qs, inck=16)
                    EM(q, lambda e, it=it, b=b: e.dma_start(
                        out=wg_u[b][:, :],
                        in_=WGUT.ap()[64 + it:65 + it, :].rearrange("r (p c) -> (r p) c", p=128)),
                        inc=qs, inck=16)
                    a_wg = p.ack(q, qs, ak)
                    W("tensor", ak, a_wg)
                    W("tensor", "asy", a_xnt)
                    if (lh, it - 2) in h_done:
                        W("tensor", "vv", h_done[(lh, it - 2)])
                    elif lh == 1 and it < 2:
                        W("tensor", "vv", h_done[(0, 62 + it)])
                    for dt in range(16):
                        EM("tensor", lambda e, dt=dt, b=b: e.matmul(
                            ps_g[b][:, :], wg_g[b][:, dt * 128:(dt + 1) * 128], xnt[:, dt * 512:(dt + 1) * 512],
                            start=(dt == 0), stop=(dt == 15)))
                    for dt in range(16):
                        gu_mm[(lh, it)] = EM("tensor", lambda e, dt=dt, b=b: e.matmul(
                            ps_u[b][:, :], wg_u[b][:, dt * 128:(dt + 1) * 128], xnt[:, dt * 512:(dt + 1) * 512],
                            start=(dt == 0), stop=(dt == 15)), inc="mm" if dt == 15 else None)
                    W("scalar", "mm", gu_mm[(lh, it)])
                    if (lh, it - 2) in h_done:
                        W("scalar", "vv", h_done[(lh, it - 2)])
                    elif lh == 1 and it < 2:
                        W("scalar", "vv", h_done[(0, 62 + it)])
                    sa = EM("scalar", lambda e, b=b: e.activation(sg[b][:, :], ps_g[b][:, :], AF.Silu), inc="aa")
                    W("vector", "aa", sa)
                    if lh == 1 and it == 0:
                        W("vector", "mm", dn_mm[(0, 3, 63)])
                    h_done[(lh, it)] = EM("vector", lambda e, it=it, b=b: e.tensor_mul(
                        h_bf[:, it * 512:(it + 1) * 512], sg[b][:, :], ps_u[b][:, :]), inc="vv")
                W("tensor", "vv", h_done[(lh, 63)])
                for dc in range(4):
                    if (lh, dc - 1) in dn_evac:
                        W("tensor", "vv", dn_evac[(lh, dc - 1)])
                    elif lh == 1 and dc == 0:
                        W("tensor", "vv", dn_evac[(0, 3)])
                    for it in range(64):
                        b = it % 2
                        q = "sync" if b == 0 else "gpsimd"
                        qs, ak = ("dsy", "asy") if b == 0 else ("dgp", "agp")
                        if it >= 2:
                            W(q, "mm", dn_mm[(lh, dc, it - 2)])
                        elif dc > 0:
                            W(q, "mm", dn_mm[(lh, dc - 1, 62 + it)])
                        EM(q, lambda e, it=it, dc=dc, b=b: e.dma_start(
                            out=wdn_t[b][:, :], in_=WDNT.ap()[it * 128:(it + 1) * 128, dc * 512:(dc + 1) * 512]),
                            inc=qs, inck=16)
                        a_wdn = p.ack(q, qs, ak)
                        W("tensor", ak, a_wdn)
                        for lt2 in range(4):
                            dn_mm[(lh, dc, it)] = EM("tensor", lambda e, it=it, b=b, lt2=lt2: e.matmul(
                                ps_dn[lt2][:, :], h_bf[:, it * 512 + lt2 * 128:it * 512 + (lt2 + 1) * 128],
                                wdn_t[b][:, :], start=(it == 0), stop=(it == 63)),
                                inc="mm" if lt2 == 3 else None)
                    W("vector", "mm", dn_mm[(lh, dc, 63)])
                    ev = None
                    for lt2 in range(4):
                        prev = o2_ack.get((lh, dc, lt2 - 1)) or o2_ack.get((lh, dc - 1, 3)) or \
                            o2_ack.get((lh - 1, 3, 3))
                        if prev:
                            W("vector", "agp", prev)
                        ev = EM("vector", lambda e, lt2=lt2: e.tensor_copy(o2_t[:, :], ps_dn[lt2][:, :]),
                                inc="vv")
                        W("gpsimd", "vv", ev)
                        EM("gpsimd", lambda e, lh=lh, lt2=lt2, dc=dc: e.dma_start(
                            out=OUT2_SP.ap()[lh * 512 + lt2 * 128:lh * 512 + (lt2 + 1) * 128,
                                             dc * 512:(dc + 1) * 512], in_=o2_t[:, :]), inc="dgp", inck=16)
                        o2_ack[(lh, dc, lt2)] = gack()
                    dn_evac[(lh, dc)] = ev
            p.barrier(dma_flush=[("gpsimd", "dgp"), ("sync", "dsy")])

        with contextlib.ExitStack() as ph:
            sb = lambda name, shape, dtype=F32: ph.enter_context(nc.sbuf_tensor(name, shape, dtype))
            psb = lambda name, shape: ph.enter_context(nc.psum_tensor(name, shape, F32))
            y_sb = sb("y_sb", [128, 32 * LC], BF16)
            owt_t = [sb(f"owt_t{i}", [128, 512], BF16) for i in range(2)]
            xn_t4 = sb("xn_t4", [128, 512])
            o2_t4 = sb("o2_t4", [128, 512])
            g_t4 = sb("g_t4", [128, 512])
            rs_t4 = sb("rs_t4", [128, 512])
            s_t4 = sb("s_t4", [128, 512])
            out_t4 = sb("out_t4", [128, 512])
            ps_o = [psb(f"ps_o{i}", [128, 512]) for i in range(8)]

            W("sync", "cc_y", 1)
            W("sync", "cc_ow", 1)

            def y_load(e):
                for arm in e.switch_core_id(n=128):
                    lcore = physical_to_logical_core_id(int(arm) % 128, 0, container_num_cores=NCORE) % NCORE
                    c0 = lcore * LC
                    e.dma_start(
                        out=y_sb[:, :].rearrange("p (et l) -> p et l", et=32),
                        in_=Y_ALL.ap()[:, c0:c0 + LC].rearrange("(et p) l -> p et l", p=128),
                    ).then_inc(p.sems["dsy"], 16)
            p.c["dsy"] += 16
            p.prog["sync"].append(y_load)
            yl_ack = sack()

            ow_mm, ph3_evac, od_ack = {}, {}, {}
            for dc in range(4):
                for et8 in range(32):
                    b = et8 % 2
                    qw = "sync" if b == 0 else "gpsimd"
                    if (dc, et8 - 2) in ow_mm:
                        W(qw, "mm", ow_mm[(dc, et8 - 2)])
                    elif dc > 0 and et8 < 2:
                        W(qw, "mm", ow_mm[(dc - 1, 30 + et8)])
                    q, qs, ak = ("sync", "dsy", "asy") if b == 0 else ("gpsimd", "dgp", "agp")
                    EM(q, lambda e, et8=et8, dc=dc, b=b: e.dma_start(
                        out=owt_t[b][:, :], in_=OWT.ap()[et8 * 128:(et8 + 1) * 128, dc * 512:(dc + 1) * 512]),
                        inc=qs, inck=16)
                    a_ow = p.ack(q, qs, ak)
                    W("tensor", ak, a_ow)
                    W("tensor", "asy", yl_ack)
                    if dc > 0 and et8 == 0:
                        W("tensor", "vv", ph3_evac[dc - 1])
                    for lt in range(8):
                        ow_mm[(dc, et8)] = EM("tensor", lambda e, et8=et8, b=b, lt=lt: e.matmul(
                            ps_o[lt][:, :], y_sb[:, et8 * LC + lt * 128:et8 * LC + (lt + 1) * 128],
                            owt_t[b][:, :], start=(et8 == 0), stop=(et8 == 31)),
                            inc="mm" if lt == 7 else None)
                last_ov = None
                for lt in range(8):
                    prev_ov = last_ov if lt > 0 else (ph3_evac.get(dc - 1) if dc > 0 else None)
                    if prev_ov:
                        W("gpsimd", "vv", prev_ov)
                    W("gpsimd", "mm", ow_mm[(dc, 31)])
                    EM("gpsimd", lambda e, lt=lt, dc=dc: e.dma_start(
                        out=xn_t4[:, :], in_=XN_SP.ap()[lt * 128:(lt + 1) * 128, dc * 512:(dc + 1) * 512]),
                        inc="dgp", inck=16)
                    EM("gpsimd", lambda e, lt=lt, dc=dc: e.dma_start(
                        out=o2_t4[:, :], in_=OUT2_SP.ap()[lt * 128:(lt + 1) * 128, dc * 512:(dc + 1) * 512]),
                        inc="dgp", inck=16)
                    EM("gpsimd", lambda e, lt=lt, dc=dc: e.dma_start(
                        out=rs_t4[:, :], in_=resid[lt * 128:(lt + 1) * 128, dc * 512:(dc + 1) * 512]),
                        inc="dgp", inck=16)
                    a_ld3 = gack()
                    W("scalar", "agp", a_ld3)
                    sg4 = EM("scalar", lambda e: e.activation(g_t4[:, :], o2_t4[:, :], AF.Silu), inc="aa")
                    W("vector", "agp", a_ld3)
                    W("vector", "aa", sg4)
                    W("vector", "mm", ow_mm[(dc, 31)])
                    prev_od = od_ack.get((dc, lt - 1)) or (od_ack.get((dc - 1, 7)) if dc > 0 else None)
                    if prev_od:
                        W("vector", "agp", prev_od)
                    EM("vector", lambda e, lt=lt: e.tensor_add(s_t4[:, :], xn_t4[:, :], ps_o[lt][:, :]))
                    EM("vector", lambda e: e.tensor_mul(s_t4[:, :], s_t4[:, :], g_t4[:, :]))
                    last_ov = EM("vector", lambda e: e.tensor_add(out_t4[:, :], rs_t4[:, :], s_t4[:, :]), inc="vv")
                    W("gpsimd", "vv", last_ov)
                    EM("gpsimd", lambda e, lt=lt, dc=dc: e.dma_start(
                        out=out[lt * 128:(lt + 1) * 128, dc * 512:(dc + 1) * 512], in_=out_t4[:, :]),
                        inc="dgp", inck=16)
                    od_ack[(dc, lt)] = gack()
                ph3_evac[dc] = last_ov

            if debug:
                for src, dst in [(XN_SP, dbg["dbg_xn"]), (ag_rn_in, dbg["dbg_rn"]),
                                 (ar_out, dbg["dbg_d1bc"]), (SK2_SP, dbg["dbg_sk2"]),
                                 (ag_y_in, dbg["dbg_y"]), (OUT2_SP, dbg["dbg_out2"]),
                                 (X_SP, dbg["dbg_x"])]:
                    EM("gpsimd", lambda e, src=src, dst=dst: e.dma_start(out=dst, in_=src.ap()),
                       inc="dgp", inck=16)
            W("gpsimd", "dgp", p.c["dgp"])
            p.barrier(dma_flush=[("gpsimd", "dgp"), ("sync", "dsy")])

        return _finish(nc, p)


def _finish(nc, p):
    with nc.Block() as block:
        @block.sync
        def _(e):
            for fn in p.prog["sync"]:
                fn(e)

        @block.gpsimd
        def _(e):
            for fn in p.prog["gpsimd"]:
                fn(e)

        @block.vector
        def _(e):
            for fn in p.prog["vector"]:
                fn(e)

        @block.scalar
        def _(e):
            for fn in p.prog["scalar"]:
                fn(e)

        @block.tensor
        def _(e):
            for fn in p.prog["tensor"]:
                fn(e)
    return nc


def make_in_maps(inputs):
    x0 = np.asarray(inputs["L0_attn_resid"]).reshape(L, D)
    norm_w = np.asarray(inputs["L0_attn_norm_w"])
    skip_w = np.asarray(inputs["L0_attn_skip_w"])
    in_w = np.asarray(inputs["L0_attn_in_w"])
    conv_w = np.asarray(inputs["L0_attn_conv_w"])
    conv_b = np.asarray(inputs["L0_attn_conv_b"])
    wd1 = np.asarray(inputs["L0_attn_wd1"])
    wd2 = np.asarray(inputs["L0_attn_wd2"])
    wd2_b = np.asarray(inputs["L0_attn_wd2_b"])
    wB = np.asarray(inputs["L0_attn_wB"])
    wC = np.asarray(inputs["L0_attn_wC"])
    A_log = np.asarray(inputs["L0_attn_A_log"])
    W_D = np.asarray(inputs["L0_attn_W_D"])
    out_w = np.asarray(inputs["L0_attn_out_w"])
    wgu = np.asarray(inputs["L0_ffn_w_gate_up"])
    wdn = np.asarray(inputs["L0_ffn_w_down"])

    nw_rep = np.ascontiguousarray(np.broadcast_to(norm_w[None, :], (128, D)).astype(np.float32))
    wguT = np.ascontiguousarray(wgu.T)
    gtile = wguT[:, :I].reshape(16, 128, 64, 128).transpose(2, 1, 0, 3).reshape(64, 262144)
    utile = wguT[:, I:].reshape(16, 128, 64, 128).transpose(2, 1, 0, 3).reshape(64, 262144)
    wgut_tiled = np.ascontiguousarray(np.concatenate([gtile, utile], 0))
    wdnT = np.ascontiguousarray(wdn.T)
    owT = np.ascontiguousarray(out_w.T)
    sel = np.zeros((32, 32 * 128), np.float32)
    for k in range(32):
        sel[k, k * 128:(k + 1) * 128] = 1.0
    sel_bf = sel.astype(ml_dtypes.bfloat16)

    in_maps = []
    for i in range(NCORE):
        ls = slice(i * LC, (i + 1) * LC)
        es = slice(i * EC, (i + 1) * EC)
        ev = np.zeros((EC, 24), np.float32)
        ev[:, 0:4] = conv_w[es]
        ev[:, 4] = conv_b[es]
        ev[:, 5] = wd2_b[es]
        ev[:, 6] = W_D[es]
        ev[:, 8:24] = A_log[es]
        in_maps.append({
            "resid": np.ascontiguousarray(x0[ls]),
            "nw_rep": nw_rep,
            "inwT": np.ascontiguousarray(in_w[es].T),
            "skwT": np.ascontiguousarray(skip_w[es].T),
            "wd1T": np.ascontiguousarray(wd1[:, es].T),
            "wd2T": np.ascontiguousarray(wd2[es].T),
            "wbcT": np.ascontiguousarray(np.concatenate([wB[:, es].T, wC[:, es].T], axis=1)),
            "evec": ev,
            "sel_in": sel_bf,
            "wguT_sl": np.ascontiguousarray(
                wgut_tiled[i * 16:(i + 1) * 16].reshape(128, 32768)),
            "wdnT_sl": np.ascontiguousarray(wdnT[i * (I // NCORE):(i + 1) * (I // NCORE)]),
            "owT_sl": np.ascontiguousarray(owT[i * (E // NCORE):(i + 1) * (E // NCORE)]),
        })
    return in_maps


def kernel(**inputs):
    in_maps = make_in_maps(inputs)
    nc = build_nc()
    res = run_bass_kernel_spmd(nc, in_maps, core_ids=list(range(NCORE)))
    outp = np.concatenate([res.results[i]["out"] for i in range(NCORE)], axis=0)
    return outp.reshape(1, L, D).astype(np.float32)
